# version 03
# baseline (speedup 1.0000x reference)
import numpy as np
import ml_dtypes
import concourse.bass as bass
import concourse.mybir as mybir
from concourse.bass_utils import run_bass_kernel_spmd

F32 = mybir.dt.float32
BF16 = mybir.dt.bfloat16
AF = mybir.ActivationFunctionType
ALU = mybir.AluOpType
AX = mybir.AxisListType
BF = ml_dtypes.bfloat16

D = 1024
DI = 2048
DST = 128
NH = 32
HD = 64
CONVD = DI + 2 * DST
DPROJ = 2 * DI + 2 * DST + NH
AH = 16
AHD = 64
WIN = 256
DFF = 2730
NEG = -1.0e38

ENGINES = ("pe", "dve", "act", "pool", "sp")


class Buf:
    __slots__ = ("name", "w", "readers")

    def __init__(self, name):
        self.name = name
        self.w = None
        self.readers = {}


class Em:

    def __init__(self):
        self.cmds = {e: [] for e in ENGINES}
        self.done = {}
        self.synced = {e: {} for e in ENGINES}
        self.nops = 0
        for e in ENGINES:
            self.done[e] = 0

    sem_alias = {}

    def op(self, eng, fn, ins=(), outs=(), dma=False, sem=None, waw=True):
        if dma:
            s = sem or ("d_" + (outs[0].name if outs else "null"))
            s = self.sem_alias.get(s, s)
        else:
            s = eng
        inc = 16 if dma else 1
        waits = {}
        syn = self.synced[eng]

        def need(dep):
            if dep is None:
                return
            ds, dv = dep
            if dv <= syn.get(ds, 0):
                return
            if dv > waits.get(ds, 0):
                waits[ds] = dv

        for b in ins:
            need(b.w)
        for b in outs:
            if waw:
                need(b.w)
            for ds, dv in b.readers.items():
                need((ds, dv))
        for ds, dv in waits.items():
            syn[ds] = dv
        self.done[s] = self.done.get(s, 0) + inc
        val = self.done[s]
        self.cmds[eng].append((waits, fn, s, inc))
        self.nops += 1
        for b in outs:
            b.w = (s, val)
            b.readers = {}
        for b in ins:
            if b.readers.get(s if False else eng, 0) < val and not dma:
                b.readers[eng] = val
            elif dma:
                if b.readers.get(s, 0) < val:
                    b.readers[s] = val

    def barrier(self):
        for e in ENGINES:
            waits = {}
            syn = self.synced[e]
            for s, dv in self.done.items():
                if s != e and dv > syn.get(s, 0):
                    waits[s] = dv
                    syn[s] = dv
            if waits:
                self.cmds[e].append((waits, None, e, 0))


class Arena:
    def __init__(self, nc, name, nbytes):
        self.t = nc.alloc_sbuf_tensor(name, [128, nbytes // 4], F32)
        self.base = 0
        self.hi = nbytes
        self.marks = []

    def mark(self):
        self.marks.append(self.base)

    def release(self):
        self.base = self.marks.pop()
        if getattr(self, "em", None) is not None:
            self.em.barrier()

    def f32(self, n):
        off = (self.base + 63) // 64 * 64
        self.base = off + n * 4
        assert self.base <= self.hi, "SBUF arena overflow"
        return self.t.ap()[:, off // 4: off // 4 + n]

    def bf16(self, n):
        off = (self.base + 63) // 64 * 64
        self.base = off + n * 2
        assert self.base <= self.hi, "SBUF arena overflow"
        return self.t.ap().bitcast(BF16)[:, off // 2: off // 2 + n]


def host_consts(L):
    c = {}
    r = np.arange(128)
    c["C_IDENT"] = np.eye(128, dtype=np.float32).astype(BF)
    c["C_EYE"] = np.eye(128, dtype=np.float32)
    c["C_LTRI1"] = (r[:, None] <= r[None, :]).astype(np.float32)
    c["C_ONESF"] = np.ones((128, 128), np.float32)
    c["C_TRI01"] = (r[:, None] <= r[None, :]).astype(np.float32)
    c["C_MASKBIG"] = np.tile((r[:, None] > r[None, :]).astype(np.float32) * NEG,
                             (1, 16)).astype(np.float32)
    c["C_ONES1"] = np.ones((1, 128), np.float32).astype(BF)
    cols = np.zeros((1, 8), np.float32)
    cols[0, :6] = [1e-5, 1.0, -0.5, -1.0, 1.0 / 2048.0, 1.0 / 1024.0]
    c["C_COLS"] = cols
    c["C_TRIU_KGT"] = (r[:, None] > r[None, :]).astype(np.float32).astype(BF)
    c["C_TRIL_KLT"] = (r[:, None] < r[None, :]).astype(np.float32).astype(BF)
    c["C_ALL1"] = np.ones((128, 128), np.float32).astype(BF)
    c["C_NEGEYE"] = (np.eye(128) * NEG).astype(BF)
    half = np.arange(0, AHD, 2, dtype=np.float64)
    inv_freq = 1.0 / 10000.0 ** (half / AHD)
    t = np.arange(L, dtype=np.float64)
    freqs = np.concatenate([t[:, None] * inv_freq[None, :]] * 2, -1)
    base_scale = (half + 0.4 * AHD) / (1.4 * AHD)
    power = (t - L // 2) / (WIN // 2)
    scale = np.concatenate([base_scale[None, :] ** power[:, None]] * 2, -1)
    cos, sin = np.cos(freqs), np.sin(freqs)
    sgn = np.concatenate([-np.ones(32), np.ones(32)])[None, :]
    c["C_CSQ"] = (cos * scale).astype(np.float32)
    c["C_SSQ"] = (sin * scale * sgn).astype(np.float32)
    c["C_CSK"] = (cos / scale).astype(np.float32)
    c["C_SSK"] = (sin / scale * sgn).astype(np.float32)
    return c


def build_program(cfg):
    L = cfg["L"]
    Bc = cfg["B"]
    depth = cfg["depth"]
    NT = L // 128
    NG = L // 512

    nc = bass.Bass("TRN2", target_bir_lowering=False)
    em = Em()
    em.sem_alias = {}
    for cn in ("C_IDENT", "C_EYE", "C_LTRI1", "C_ONESF", "C_TRI01",
               "C_TRIU_KGT", "C_TRIL_KLT", "C_ALL1", "C_NEGEYE", "C_ONES1",
               "ccol", "maskh", "convb_c", "convw_c", "dtb_r", "Aneg_r",
               "D_r"):
        em.sem_alias["d_" + cn] = "d_const"
    for old, new in (("m1x0", "ld0"), ("m1x1", "ld1"), ("m1x2", "ld2"),
                     ("m1x3", "ld3"),
                     ("a_h0", "ld0"), ("a_h1", "ld1"), ("a_h2", "ld2"),
                     ("a_h3", "ld3"), ("a_h4", "ld4"), ("a_h5", "ld5"),
                     ("a_h6", "ld6"), ("a_h7", "ld7"),
                     ("a_hs0", "ldh0"), ("a_hs1", "ldh1"), ("a_hs2", "ldh2"),
                     ("a_hs3", "ldh3"),
                     ("f_h0", "ld0"), ("f_h1", "ld1"),
                     ("f_hs0", "ldh0"), ("f_hs1", "ldh1"),
                     ("f2_h0", "ld0"), ("f2_h1", "ld1"),
                     ("f2_x10", "ld4"), ("f2_x11", "ld5"),
                     ("p2y0", "ld0"), ("p2y1", "ld1"), ("p2z0", "ld2"),
                     ("p2z1", "ld3"), ("p2x00", "ld4"), ("p2x01", "ld5"),
                     ("m1bx0", "ld0"), ("m1bx1", "ld1"),
                     ("m1bdt0", "ldh0"), ("m1bdt1", "ldh1")):
        em.sem_alias["d_" + old] = "d_" + new
    for old, new in (("m1z0", "st0"), ("m1z1", "st1"),
                     ("m1dtst0", "st2"), ("m1dtst1", "st3"),
                     ("m1by0", "st0"), ("m1by1", "st1"),
                     ("m1bct0", "st2"), ("m1bct1", "st3"),
                     ("p2x10", "st0"), ("p2x11", "st1"),
                     ("p2h00", "st2"), ("p2h01", "st3"),
                     ("a_hn0", "st0"), ("a_hn1", "st1"),
                     ("f_g0", "st0"), ("f_g1", "st1"), ("f_brs", "st2"),
                     ("f2_hn0", "st0"), ("f2_hn1", "st1")):
        em.sem_alias["d_st_" + old] = "d_st_" + new
    bufs = {}

    def B(name):
        b = bufs.get(name)
        if b is None:
            b = bufs[name] = Buf(name)
        return b

    def dma(q, out, in_, outs=(), ins=(), sem=None, **kw):
        em.op(q, lambda e: e.dma_start(out=out, in_=in_, **kw),
              ins=ins, outs=outs, dma=True, sem=sem)

    def store(q, out, in_, srcbuf, hbuf, **kw):
        em.op(q, lambda e: e.dma_start(out=out, in_=in_, **kw),
              ins=[srcbuf], outs=[hbuf], dma=True,
              sem="d_st_" + srcbuf.name, waw=False)

    consts = host_consts(L)
    ins = {}

    def inp(name, shape, dtype=F32):
        ap = nc.dram_tensor(name, list(shape), dtype, kind="ExternalInput").ap()
        ins[name] = ap
        return ap

    u = inp("L0_attn_u", (Bc, L, D))
    W_in = inp("L0_attn_W_in", (DPROJ, D))
    conv_w = inp("L0_attn_conv_w", (CONVD, 1, 4))
    conv_b = inp("L0_attn_conv_b", (CONVD,))
    dt_bias = inp("L0_attn_dt_bias", (NH,))
    A_log = inp("L0_attn_A_log", (NH,))
    D_param = inp("L0_attn_D_param", (NH,))
    norm_w = inp("L0_attn_norm_w", (DI,))
    W_out = inp("L0_attn_W_out", (D, DI))
    Wqkv = inp("L0_ffn_Wqkv", (6, D, 3 * D))
    q_scale = inp("L0_ffn_q_scale", (6, AHD))
    k_scale = inp("L0_ffn_k_scale", (6, AHD))
    Wo = inp("L0_ffn_Wo", (6, D, D))
    ln_g = inp("L0_ffn_ln_g", (6, D))
    ln_b = inp("L0_ffn_ln_b", (6, D))
    W1 = inp("L0_ffn_W1", (6, D, 2 * DFF))
    b1 = inp("L0_ffn_b1", (6, 2 * DFF))
    W2 = inp("L0_ffn_W2", (6, DFF, D))
    b2 = inp("L0_ffn_b2", (6, D))
    for name, arr in consts.items():
        inp(name, arr.shape, BF16 if arr.dtype == BF else F32)

    OUT = nc.dram_tensor("OUT", [Bc, L, D], F32, kind="ExternalOutput").ap()

    def scratch(name, shape, dtype=F32):
        return nc.dram_tensor(name, list(shape), dtype, kind="Internal").ap()

    x1_h = scratch("x1_h", (Bc, L, D))
    hA_h = scratch("hA_h", (Bc, L, D))
    hB_h = scratch("hB_h", (Bc, L, D))
    y_h = scratch("y_h", (Bc, L, DI), BF16)
    z_h = scratch("z_h", (Bc, L, DI), BF16)
    g_h = scratch("gact_h", (Bc, DFF, L), BF16)
    cumT_h = scratch("cumT_h", (2, NH, 128))
    xbc_h = scratch("xbc_h", (Bc, 18, 128, L), BF16)
    dtraw_h = scratch("dtraw_h", (Bc, L // 128, 128, NH))
    brow_h = scratch("brow_h", (5632,))
    MASKBIG_h = scratch("maskbig_h", (128, 16 * 128))

    qkvtap_h = scratch("qkvtap_h", (3, L // 128, 128, 1024), BF16)
    avtap_h = scratch("avtap_h", (L // 128, 128, 1024), BF16)
    attap_h = scratch("attap_h", (3, 128, 512), BF16)
    dbg = {}
    for name in cfg.get("debug", ()):
        src = {"x1": x1_h, "h0": hA_h, "y": y_h, "z": z_h, "hB": hB_h,
               "gact": g_h, "qkvtap": qkvtap_h, "avtap": avtap_h,
               "attap": attap_h}[name]
        dbg[name] = (nc.dram_tensor("DBG_" + name, list(src.tensor.shape),
                                    src.dtype, kind="ExternalOutput").ap(), src)

    sb = {}

    def sbt(name, shape, dtype=F32):
        t = nc.alloc_sbuf_tensor(name, list(shape), dtype)
        sb[name] = t
        return t

    ident = sbt("ident", (128, 128), BF16)
    eye = sbt("eye", (128, 128), F32)
    ltri1 = sbt("ltri1", (128, 128), F32)
    onesf = sbt("onesf", (128, 128), F32)
    tri01 = sbt("tri01", (128, 128), F32)
    ones1 = sbt("ones1", (1, 128), BF16)
    triu_kgt = sbt("triu_kgt", (128, 128), BF16)
    tril_klt = sbt("tril_klt", (128, 128), BF16)
    all1 = sbt("all1", (128, 128), BF16)
    negeye = sbt("negeye", (128, 128), BF16)
    convb_c = sbt("convb_c", (128, 18), F32)
    convw_c = sbt("convw_c", (128, 4, 18), F32)
    dtb_r = sbt("dtb_r", (128, NH), F32)
    Aneg_r = sbt("Aneg_r", (128, NH), F32)
    D_r = sbt("D_r", (128, NH), F32)
    ccol = sbt("ccol", (128, 8), F32)

    arena = Arena(nc, "arena", 202 * 1024)
    arena.em = em
    psum = [nc.alloc_psum_tensor(f"ps{i}", [128, 512], F32) for i in range(8)]
    psb = [t.ap().bitcast(BF16) for t in psum]
    PS = [B(f"ps{i}") for i in range(8)]

    for name, tile in (("C_IDENT", ident), ("C_EYE", eye), ("C_LTRI1", ltri1),
                       ("C_ONESF", onesf), ("C_TRI01", tri01),
                       ("C_TRIU_KGT", triu_kgt), ("C_TRIL_KLT", tril_klt),
                       ("C_ALL1", all1), ("C_NEGEYE", negeye)):
        dma("sp", tile[:], ins[name][:, :], outs=[B(name)])
    dma("sp", ones1[:], ins["C_ONES1"][:, :], outs=[B("C_ONES1")])
    dma("sp", ccol[:], ins["C_COLS"][0:1, :].partition_broadcast(128).squeeze(1),
        outs=[B("ccol")], allow_slow_non_contiguous=True)
    cEPS = ccol.ap()[:, 0:1]
    cONE = ccol.ap()[:, 1:2]
    cNHALF = ccol.ap()[:, 2:3]
    cNONE = ccol.ap()[:, 3:4]
    cIDI = ccol.ap()[:, 4:5]
    cID = ccol.ap()[:, 5:6]
    dma("sp", MASKBIG_h[:, :], ins["C_MASKBIG"][:, :], outs=[B("maskh")])
    dma("sp", convb_c[:], conv_b.rearrange("(c p) -> p c", p=128),
        outs=[B("convb_c")], allow_slow_non_contiguous=True)
    cw_flat = conv_w.rearrange("e o k -> (e o k)")
    for k in range(4):
        dma("sp", convw_c[:, k, :],
            cw_flat[k::4].rearrange("(c p) -> p c", p=128),
            outs=[B("convw_c")], allow_slow_non_contiguous=True)
    for row, tile, bn in ((dt_bias, dtb_r, "dtb_r"), (A_log, Aneg_r, "Aneg_r"),
                          (D_param, D_r, "D_r")):
        dma("sp", tile[:], row.unsqueeze(0).partition_broadcast(128).squeeze(1),
            outs=[B(bn)], allow_slow_non_contiguous=True)
    em.barrier()
    em.op("act", lambda e: e.activation(Aneg_r[:], Aneg_r[:], AF.Exp),
          ins=[B("Aneg_r")], outs=[B("Aneg_r")])
    em.op("dve", lambda e: e.tensor_single_scalar(Aneg_r[:], Aneg_r[:], -1.0,
                                                  ALU.mult),
          ins=[B("Aneg_r")], outs=[B("Aneg_r")])
    em.barrier()

    lnscrt = arena.f32(4 + 1024)
    lns1, lns2, lnmu, lnrs = (lnscrt[:, i:i + 1] for i in range(4))
    lnsq = lnscrt[:, 4:4 + 1024]
    LNS = [B("lns1"), B("lns2"), B("lnmu"), B("lnrs"), B("lnsq")]

    def ln_tile(xap, outap, xbuf, obuf, n=1024, eng_apply="dve"):
        em.op("dve", lambda e: e.reduce_sum(lns1, xap, AX.X),
              ins=[xbuf], outs=[LNS[0]])
        em.op("act", lambda e: e.activation(lnsq[:, 0:n], xap, AF.Square,
                                            accum_out=lns2),
              ins=[xbuf], outs=[LNS[4], LNS[1]])
        em.op("dve", lambda e: e.tensor_single_scalar(lnmu, lns1, 1.0 / n,
                                                      ALU.mult),
              ins=[LNS[0]], outs=[LNS[2]])
        em.op("dve", lambda e: e.tensor_tensor(lnrs, lnmu, lnmu, ALU.mult),
              ins=[LNS[2]], outs=[LNS[3]])
        em.op("dve", lambda e: e.scalar_tensor_tensor(lnrs, lns2, 1.0 / n, lnrs,
                                                      ALU.mult, ALU.subtract),
              ins=[LNS[1], LNS[3]], outs=[LNS[3]])
        em.op("act", lambda e: e.activation(lnrs, lnrs, AF.Ln, bias=cEPS),
              ins=[LNS[3]], outs=[LNS[3]])
        em.op("act", lambda e: e.activation(lnrs, lnrs, AF.Exp, scale=cNHALF),
              ins=[LNS[3]], outs=[LNS[3]])
        em.op(eng_apply, lambda e: e.tensor_scalar(outap, xap, lnmu, lnrs,
                                                   ALU.subtract, ALU.mult),
              ins=[LNS[2], LNS[3], xbuf], outs=[obuf])

    def transpose_pack(src_ap, src_buf, nchunk, dst3, dst_buf, tcol=None):
        def mkt(e):
            last = None
            for k in range(nchunk):
                bank = 6 + (k // 4) % 2
                col = (k % 4) * 128
                if k % 8 == 0 and k > 0:
                    pass
                last = e.transpose(psb[bank][:, col:col + 128],
                                   src_ap[:, k * 128:(k + 1) * 128], ident[:])
            return last
        for g0 in range(0, nchunk, 8):
            g1 = min(g0 + 8, nchunk)
            def mkt2(g0=g0, g1=g1):
                def f(e):
                    last = None
                    for k in range(g0, g1):
                        j = k - g0
                        bank = 6 + j // 4
                        col = (j % 4) * 128
                        last = e.transpose(psb[bank][:, col:col + 128],
                                           src_ap[:, k * 128:(k + 1) * 128],
                                           ident[:])
                    return last
                return f
            em.op("pe", mkt2(), ins=[src_buf, B("C_IDENT")], outs=[PS[6], PS[7]])
            def mkev(g0=g0, g1=g1):
                def f(e):
                    last = None
                    n1 = min(4, g1 - g0)
                    last = e.tensor_copy(dst3[:, g0:g0 + n1, :],
                                         psb[6][:, 0:n1 * 128].rearrange(
                                             "p (k c) -> p k c", k=n1))
                    if g1 - g0 > 4:
                        n2 = g1 - g0 - 4
                        last = e.tensor_copy(dst3[:, g0 + 4:g0 + 4 + n2, :],
                                             psb[7][:, 0:n2 * 128].rearrange(
                                                 "p (k c) -> p k c", k=n2))
                    return last
                return f
            em.op("dve", mkev(), ins=[PS[6], PS[7]], outs=[dst_buf])

    def prep_rowmajor_T(W, rows, wT3, wTbuf, tag):
        arena.mark()
        wtmp = arena.bf16(2 * 1024)
        wt = [wtmp[:, 0:1024], wtmp[:, 1024:2048]]
        WT = [B(tag + "0"), B(tag + "1")]
        nrow = (rows + 127) // 128
        for rc in range(nrow):
            r0, r1 = rc * 128, min(rc * 128 + 128, rows)
            nr = r1 - r0
            bb, tile = WT[rc % 2], wt[rc % 2]
            dma("pool", tile[0:nr, :], W[r0:r1, :], outs=[bb])
            def mkt(tile=tile, nr=nr):
                def f(e):
                    last = None
                    for k in range(8):
                        bank = 6 + k // 4
                        col = (k % 4) * 128
                        last = e.transpose(psb[bank][:, col:col + nr],
                                           tile[0:nr, k * 128:(k + 1) * 128],
                                           ident[0:nr, 0:nr])
                    return last
                return f
            em.op("pe", mkt(), ins=[bb, B("C_IDENT")], outs=[PS[6], PS[7]])
            def mkev(r0=r0, nr=nr):
                def f(e):
                    for half in range(2):
                        e.tensor_copy(
                            wT3[:, 4 * half:4 * half + 4, r0:r0 + nr],
                            psb[6 + half][:, 0:512].rearrange(
                                "p (k c) -> p k c", k=4)[:, :, 0:nr])
                    return e.engine_nop()
                return f
            em.op("dve", mkev(), ins=[PS[6], PS[7]], outs=[wTbuf])
        arena.release()

    def phase_m1a():
        arena.mark()
        winT = arena.bf16(8 * DPROJ)
        winT_r = winT.rearrange("p (k e) -> p k e", k=8)
        prep_rowmajor_T(W_in, DPROJ, winT_r, B("winT"), "wi")
        convdiag = arena.bf16(72 * 128)
        convdiag_r = convdiag.rearrange("p (c n) -> p c n", c=72)
        for ec in range(18):
            for k in range(4):
                def mkd(ec=ec, k=k):
                    def f(e):
                        return e.tensor_scalar(convdiag_r[:, ec * 4 + k, :],
                                               eye[:], convw_c[:, k, ec:ec + 1],
                                               None, ALU.mult)
                    return f
                em.op("dve", mkd(), ins=[B("C_EYE"), B("convw_c")],
                      outs=[B("convdiag")])
        ring = arena.bf16(18 * 516)
        ring_r = ring.rearrange("p (c n) -> p c n", c=18)
        RING = [B(f"ring{i}") for i in range(18)]
        xt = [arena.f32(1024) for _ in range(4)]
        XT = [B(f"m1x{i}") for i in range(4)]
        xn = [arena.bf16(1024) for _ in range(2)]
        XN = [B("m1xn0"), B("m1xn1")]
        xnT = [arena.bf16(8 * 512) for _ in range(2)]
        xnT_r = [t.rearrange("p (k t) -> p k t", k=8) for t in xnT]
        XNT = [B("m1xnT0"), B("m1xnT1")]
        zst = [arena.bf16(DI) for _ in range(2)]
        ZST = [B("m1z0"), B("m1z1")]
        dtst = [arena.f32(NH) for _ in range(2)]
        DTST = [B("m1dtst0"), B("m1dtst1")]

        for b in range(Bc):
            em.op("pool", lambda e: e.memset(ring_r[:, :, 0:3], 0.0), outs=RING)
            for g in range(NG):
                xnb, XNB = xn[g % 2], XN[g % 2]
                xnTb_r, XNTB = xnT_r[g % 2], XNT[g % 2]
                for i in range(4):
                    t0 = g * 512 + i * 128
                    dma("sp", xt[i][:, :], u[b, t0:t0 + 128, :], outs=[XT[i]])
                for i in range(4):
                    ln_tile(xt[i][:, :], xnb[:, :], XT[i], XNB)
                    transpose_pack(xnb, XNB, 8,
                                   xnTb_r[:, :, i * 128:(i + 1) * 128], XNTB)

                for i in range(4):
                    ti = g * 4 + i
                    def mkz(i=i, xnTb_r=xnTb_r):
                        def f(e):
                            last = None
                            for k in range(8):
                                st = xnTb_r[:, k, i * 128:(i + 1) * 128]
                                for s in range(4):
                                    e.matmul(psum[s][:, :], st,
                                             winT_r[:, k, s * 512:(s + 1) * 512],
                                             start=(k == 0), stop=(k == 7))
                                last = e.matmul(psum[4][:, 0:32], st,
                                                winT_r[:, k, DPROJ - 32:DPROJ],
                                                start=(k == 0), stop=(k == 7))
                            return last
                        return f
                    em.op("pe", mkz(), ins=[XNTB, B("winT")],
                          outs=[PS[0], PS[1], PS[2], PS[3], PS[4]])
                    zb, ZB = zst[i % 2], ZST[i % 2]
                    def mksilu(zb=zb):
                        def f(e):
                            last = None
                            for s in range(4):
                                last = e.activation(zb[:, s * 512:(s + 1) * 512],
                                                    psum[s][:, :], AF.Silu)
                            return last
                        return f
                    em.op("act", mksilu(), ins=[PS[0], PS[1], PS[2], PS[3]],
                          outs=[ZB])
                    t0 = g * 512 + i * 128
                    store("sp", z_h[b, t0:t0 + 128, :], zb[:, :], ZB, B("z_h"))
                    db, DB = dtst[i % 2], DTST[i % 2]
                    def mkdtev(db=db):
                        def f(e):
                            return e.tensor_copy(db[:, :], psum[4][:, 0:32])
                        return f
                    em.op("dve", mkdtev(), ins=[PS[4]], outs=[DB])
                    store("sp", dtraw_h[b, ti, :, :], db[:, :], DB, B("dtraw_h"))

                for ec in range(18):
                    xb_ = 4 + ec % 2
                    cb_ = 6 + ec % 2
                    def mkx(ec=ec, xnTb_r=xnTb_r, xb_=xb_):
                        def f(e):
                            last = None
                            for k in range(8):
                                last = e.matmul(
                                    psum[xb_][:, :],
                                    winT_r[:, k, DI + ec * 128:DI + ec * 128 + 128],
                                    xnTb_r[:, k, :],
                                    start=(k == 0), stop=(k == 7))
                            return last
                        return f
                    em.op("pe", mkx(), ins=[XNTB, B("winT")], outs=[PS[xb_]])
                    def mkrev(ec=ec, xb_=xb_):
                        def f(e):
                            return e.tensor_copy(ring_r[:, ec, 3:515],
                                                 psum[xb_][:, :])
                        return f
                    em.op("dve", mkrev(), ins=[PS[xb_]], outs=[RING[ec]])
                    def mkcv(ec=ec, cb_=cb_):
                        def f(e):
                            last = None
                            for k in range(4):
                                last = e.matmul(psum[cb_][:, :],
                                                convdiag_r[:, ec * 4 + k, :],
                                                ring_r[:, ec, k:k + 512],
                                                start=(k == 0), stop=(k == 3))
                            return last
                        return f
                    em.op("pe", mkcv(), ins=[RING[ec], B("convdiag")],
                          outs=[PS[cb_]])
                    def mktl(ec=ec):
                        def f(e):
                            return e.tensor_copy(ring_r[:, ec, 0:3],
                                                 ring_r[:, ec, 512:515])
                        return f
                    em.op("pool", mktl(), ins=[RING[ec]], outs=[RING[ec]])
                    def mks(ec=ec, cb_=cb_):
                        def f(e):
                            return e.activation(ring_r[:, ec, 3:515],
                                                psum[cb_][:, :], AF.Silu,
                                                bias=convb_c[:, ec:ec + 1])
                        return f
                    em.op("act", mks(), ins=[PS[cb_], B("convb_c"), RING[ec]],
                          outs=[RING[ec]])
                    def mkso(ec=ec, b=b, g=g):
                        def f(e):
                            return e.dma_start(
                                out=xbc_h[b, ec, :, g * 512:(g + 1) * 512],
                                in_=ring_r[:, ec, 3:515])
                        return f
                    em.op("sp", mkso(), ins=[RING[ec]], outs=[B("xbc_h")],
                          dma=True, sem=f"d_st_ring{ec}", waw=False)
        em.barrier()
        arena.release()

    def phase_m1b():
        arena.mark()
        ddiag = arena.bf16(32 * 128)
        ddiag_r = ddiag.rearrange("p (h n) -> p h n", h=32)
        for h in range(NH):
            def mkd(h=h):
                def f(e):
                    return e.tensor_scalar(ddiag_r[:, h, :], eye[:],
                                           D_r[:, h:h + 1], None, ALU.mult)
                return f
            em.op("dve", mkd(), ins=[B("C_EYE"), B("D_r")], outs=[B("ddiag")])
        xc = [arena.bf16(18 * 128) for _ in range(2)]
        xc_r = [t.rearrange("p (c n) -> p c n", c=18) for t in xc]
        XC = [B("m1bx0"), B("m1bx1")]
        s_f32 = arena.f32(DI)
        SF = B("m1bsf")
        s_bf = arena.bf16(DI)
        SB = B("m1bsb")
        dts = [arena.f32(8 * NH) for _ in range(2)]
        dts_r = [t.rearrange("p (a h) -> p a h", a=8) for t in dts]
        DTS = [B("m1bdt0"), B("m1bdt1")]
        cumT_sb = [arena.f32(128) for _ in range(2)]
        CUMT = [B("m1bct0"), B("m1bct1")]
        cumrep = [arena.f32(2048) for _ in range(2)]
        CUMREP = [B("m1bcr0"), B("m1bcr1")]
        preL = [arena.bf16(2048) for _ in range(2)]
        PREL = [B("m1bpl0"), B("m1bpl1")]
        expL = [arena.bf16(2048) for _ in range(2)]
        EXPL = [B("m1bel0"), B("m1bel1")]
        mprime = arena.bf16(4096)
        mprime_r = mprime.rearrange("p (h t) -> p h t", h=32)
        MPR = B("m1bmp")
        xTt = [arena.bf16(2048) for _ in range(2)]
        XTT = [B("m1bxT0"), B("m1bxT1")]
        xTw = arena.bf16(2048)
        XTW = B("m1bxw")
        gm = arena.bf16(128)
        GM = B("m1bgm")
        bT = arena.bf16(128)
        BT = B("m1bbT")
        t1c = arena.f32(2048)
        T1C = B("m1bt1")
        yout = [arena.bf16(DI) for _ in range(2)]
        YOUT = [B("m1by0"), B("m1by1")]

        for b in range(Bc):
            em.op("dve", lambda e: e.memset(s_f32, 0.0), outs=[SF])
            em.op("pool", lambda e: e.memset(s_bf, 0.0), outs=[SB])
            for ci in range(L // 128):
                c0 = ci * 128
                xcb, xcb_r, XCB = xc[ci % 2], xc_r[ci % 2], XC[ci % 2]
                dtr, DTR = dts_r[ci % 2], DTS[ci % 2]
                for ec in range(18):
                    def mkl(ec=ec, b=b, c0=c0, xcb_r=xcb_r):
                        def f(e):
                            return e.dma_start(out=xcb_r[:, ec, :],
                                               in_=xbc_h[b, ec, :, c0:c0 + 128])
                        return f
                    em.op("sp", mkl(), ins=[B("xbc_h")], outs=[XCB], dma=True)
                def mkld(b=b, ci=ci, dtr=dtr):
                    def f(e):
                        return e.dma_start(out=dtr[:, 0, :],
                                           in_=dtraw_h[b, ci, :, :])
                    return f
                em.op("sp", mkld(), ins=[B("dtraw_h")], outs=[DTR], dma=True)

                raw = dtr[:, 0, :]
                dtv = dtr[:, 1, :]
                dtA = dtr[:, 2, :]
                cum = dtr[:, 3, :]
                ecum = dtr[:, 4, :]
                wv = dtr[:, 5, :]
                ct = dtr[:, 6, :]
                ecq = dtr[:, 7, :]
                em.op("dve", lambda e, raw=raw: e.tensor_tensor(
                    raw, raw, dtb_r[:], ALU.add),
                    ins=[DTR, B("dtb_r")], outs=[DTR])
                em.op("act", lambda e, raw=raw, dtv=dtv: e.activation(
                    dtv, raw, AF.Abs), ins=[DTR], outs=[DTR])
                em.op("act", lambda e, dtv=dtv: e.activation(
                    dtv, dtv, AF.Exp, scale=cNONE), ins=[DTR], outs=[DTR])
                em.op("act", lambda e, dtv=dtv: e.activation(
                    dtv, dtv, AF.Ln, bias=cONE), ins=[DTR], outs=[DTR])
                em.op("dve", lambda e, raw=raw, dtv=dtv: e.scalar_tensor_tensor(
                    dtv, raw, 0.0, dtv, ALU.max, ALU.add),
                    ins=[DTR], outs=[DTR])
                em.op("dve", lambda e, dtv=dtv, dtA=dtA: e.tensor_tensor(
                    dtA, dtv, Aneg_r[:], ALU.mult),
                    ins=[DTR, B("Aneg_r")], outs=[DTR])
                def mkcum(dtA=dtA):
                    def f(e):
                        e.matmul(psum[4][:, 64:96], ltri1[:], dtA,
                                 start=True, stop=True)
                        return e.matmul(psum[4][:, 96:128], onesf[:], dtA,
                                        start=True, stop=True)
                    return f
                em.op("pe", mkcum(), ins=[DTR, B("C_LTRI1"), B("C_ONESF")],
                      outs=[PS[4]])
                em.op("dve", lambda e, cum=cum: e.tensor_copy(
                    cum, psum[4][:, 64:96]), ins=[PS[4]], outs=[DTR])
                em.op("act", lambda e, ecq=ecq: e.activation(
                    ecq, psum[4][:, 96:128], AF.Exp), ins=[PS[4]], outs=[DTR])
                em.op("act", lambda e, ecum=ecum, cum=cum: e.activation(
                    ecum, cum, AF.Exp), ins=[DTR], outs=[DTR])
                em.op("dve", lambda e, wv=wv: e.tensor_copy(
                    wv, psum[4][:, 96:128]), ins=[PS[4]], outs=[DTR])
                em.op("dve", lambda e, wv=wv, cum=cum: e.tensor_tensor(
                    wv, wv, cum, ALU.subtract), ins=[DTR], outs=[DTR])
                em.op("act", lambda e, wv=wv: e.activation(
                    wv, wv, AF.Exp), ins=[DTR], outs=[DTR])
                em.op("dve", lambda e, wv=wv, dtv=dtv: e.tensor_tensor(
                    wv, wv, dtv, ALU.mult), ins=[DTR], outs=[DTR])
                em.op("act", lambda e, ct=ct, dtv=dtv: e.activation(
                    ct, dtv, AF.Ln), ins=[DTR], outs=[DTR])
                em.op("dve", lambda e, ct=ct, cum=cum: e.tensor_tensor(
                    ct, ct, cum, ALU.subtract), ins=[DTR], outs=[DTR])
                def mkct2(cum=cum):
                    def f(e):
                        return e.matmul(psum[5][0:32, 0:128], cum, eye[:],
                                        start=True, stop=True)
                    return f
                em.op("pe", mkct2(), ins=[DTR, B("C_EYE")], outs=[PS[5]])
                cts, CTS = cumT_sb[ci % 2], CUMT[ci % 2]
                def mkcte(cts=cts):
                    def f(e):
                        return e.tensor_copy(cts[0:32, :], psum[5][0:32, 0:128])
                    return f
                em.op("dve", mkcte(), ins=[PS[5]], outs=[CTS])
                store("sp", cumT_h[ci % 2, :, :], cts[0:32, :], CTS,
                      B(f"cumTh{ci % 2}"))

                xb, XB = xTt[ci % 2], XTT[ci % 2]
                def mkxt(xcb_r=xcb_r):
                    def f(e):
                        last = None
                        for cch in range(16):
                            bank = 6 + cch // 8
                            col = (cch % 8) * 128
                            last = e.transpose(psb[bank][:, col:col + 128],
                                               xcb_r[:, cch, :], ident[:])
                        return last
                    return f
                em.op("pe", mkxt(), ins=[XCB, B("C_IDENT")], outs=[PS[6], PS[7]])
                def mkxe(xb=xb):
                    def f(e):
                        e.tensor_copy(xb[:, 0:1024], psb[6][:, 0:1024])
                        return e.tensor_copy(xb[:, 1024:2048], psb[7][:, 0:1024])
                    return f
                em.op("dve", mkxe(), ins=[PS[6], PS[7]], outs=[XB])
                def mkxw(xb=xb, wv=wv):
                    def f(e):
                        return e.tensor_tensor(
                            xTw.rearrange("p (h d) -> p h d", h=32),
                            xb.rearrange("p (h d) -> p h d", h=32),
                            wv.unsqueeze(2).broadcast_to([128, 32, 64]),
                            ALU.mult)
                    return f
                em.op("pool", mkxw(), ins=[XB, DTR], outs=[XTW])
                def mkbt(xcb_r=xcb_r):
                    def f(e):
                        return e.transpose(psb[5][:, 0:128], xcb_r[:, 16, :],
                                           ident[:])
                    return f
                em.op("pe", mkbt(), ins=[XCB, B("C_IDENT")], outs=[PS[5]])
                em.op("dve", lambda e: e.tensor_copy(bT[:, :], psb[5][:, 0:128]),
                      ins=[PS[5]], outs=[BT])
                def mkg(xcb_r=xcb_r):
                    def f(e):
                        return e.matmul(psum[5][:, 128:256], xcb_r[:, 16, :],
                                        xcb_r[:, 17, :], start=True, stop=True)
                    return f
                em.op("pe", mkg(), ins=[XCB, BT], outs=[PS[5]])
                em.op("dve", lambda e: e.tensor_tensor(
                    gm[:, :], psum[5][:, 128:256], tri01[:], ALU.mult),
                    ins=[PS[5], B("C_TRI01")], outs=[GM])
                for half in range(2):
                    cr, CR = cumrep[half], CUMREP[half]
                    pl, PL = preL[half], PREL[half]
                    el, EL = expL[half], EXPL[half]
                    for h in range(16):
                        hh = half * 16 + h
                        def mkbc(hh=hh, h=h, ci=ci, cr=cr):
                            def f(e):
                                return e.dma_start(
                                    out=cr[:, h * 128:(h + 1) * 128],
                                    in_=cumT_h[ci % 2, hh, :].unsqueeze(0)
                                    .partition_broadcast(128).squeeze(1))
                            return f
                        em.op("sp", mkbc(), ins=[B(f"cumTh{ci % 2}")],
                              outs=[CR], dma=True, waw=False,
                              sem="d_cr%d" % half)
                    em.op("pool", lambda e, cr=cr: e.dma_start(
                        out=cr[:, :], in_=MASKBIG_h[:, 0:2048],
                        accum_op=ALU.add),
                        ins=[B("maskh")], outs=[CR], dma=True,
                        sem="d_g_cr%d" % half)
                    def mkpre(half=half, ct=ct, pl=pl, cr=cr):
                        def f(e):
                            return e.tensor_tensor(
                                pl.rearrange("p (h t) -> p h t", h=16),
                                cr.rearrange("p (h t) -> p h t", h=16),
                                ct[:, half * 16:half * 16 + 16].unsqueeze(2)
                                .broadcast_to([128, 16, 128]),
                                ALU.add)
                        return f
                    em.op("dve", mkpre(), ins=[CR, DTR], outs=[PL])
                    em.op("act", lambda e, el=el, pl=pl: e.activation(
                        el[:, :], pl[:, :], AF.Exp), ins=[PL], outs=[EL])
                    def mkmp(half=half, el=el):
                        def f(e):
                            return e.tensor_tensor(
                                mprime_r[:, half * 16:half * 16 + 16, :],
                                el.rearrange("p (h t) -> p h t", h=16),
                                gm.unsqueeze(1).broadcast_to([128, 16, 128]),
                                ALU.mult)
                        return f
                    em.op("pool", mkmp(), ins=[EL, GM], outs=[MPR])
                def mky(xb=xb):
                    def f(e):
                        last = None
                        for h in range(NH):
                            bank = h // 8
                            col = (h % 8) * 64
                            e.matmul(psum[bank][:, col:col + 64],
                                     mprime_r[:, h, :],
                                     xb[:, h * 64:(h + 1) * 64],
                                     start=True, stop=False)
                            last = e.matmul(psum[bank][:, col:col + 64],
                                            ddiag_r[:, h, :],
                                            xb[:, h * 64:(h + 1) * 64],
                                            start=False, stop=True)
                        return last
                    return f
                em.op("pe", mky(), ins=[MPR, XB, B("ddiag")],
                      outs=[PS[0], PS[1], PS[2], PS[3]])
                def mkzm(xcb_r=xcb_r):
                    def f(e):
                        last = None
                        for h in range(NH):
                            bank = 4 + h // 8
                            col = (h % 8) * 64
                            last = e.matmul(psum[bank][:, col:col + 64],
                                            xcb_r[:, 17, :],
                                            s_bf[:, h * 64:(h + 1) * 64],
                                            start=True, stop=True)
                        return last
                    return f
                em.op("pe", mkzm(), ins=[XCB, SB],
                      outs=[PS[4], PS[5], PS[6], PS[7]])
                yo, YO = yout[ci % 2], YOUT[ci % 2]
                def mkt1(ecum=ecum):
                    def f(e):
                        last = None
                        for q in range(4):
                            last = e.tensor_tensor(
                                t1c.rearrange("p (h d) -> p h d", h=32)[
                                    :, q * 8:(q + 1) * 8, :],
                                psum[4 + q][:, :].rearrange(
                                    "p (h d) -> p h d", h=8),
                                ecum[:, q * 8:(q + 1) * 8].unsqueeze(2)
                                .broadcast_to([128, 8, 64]),
                                ALU.mult)
                        return last
                    return f
                em.op("dve", mkt1(), ins=[PS[4], PS[5], PS[6], PS[7], DTR],
                      outs=[T1C])
                def mkyf(yo=yo):
                    def f(e):
                        last = None
                        for q in range(4):
                            last = e.tensor_tensor(yo[:, q * 512:(q + 1) * 512],
                                                   t1c[:, q * 512:(q + 1) * 512],
                                                   psum[q][:, :], ALU.add)
                        return last
                    return f
                em.op("dve", mkyf(), ins=[T1C, PS[0], PS[1], PS[2], PS[3]],
                      outs=[YO])
                store("sp", y_h[b, c0:c0 + 128, :], yo[:, :], YO, B("y_h"))
                def mksm():
                    def f(e):
                        last = None
                        for h in range(NH):
                            bank = 4 + h // 8
                            col = (h % 8) * 64
                            last = e.matmul(psum[bank][:, col:col + 64],
                                            bT[:, :],
                                            xTw[:, h * 64:(h + 1) * 64],
                                            start=True, stop=True)
                        return last
                    return f
                em.op("pe", mksm(), ins=[BT, XTW],
                      outs=[PS[4], PS[5], PS[6], PS[7]])
                def mkdec(ecq=ecq):
                    def f(e):
                        return e.tensor_tensor(
                            s_f32.rearrange("p (h d) -> p h d", h=32),
                            s_f32.rearrange("p (h d) -> p h d", h=32),
                            ecq.unsqueeze(2).broadcast_to([128, 32, 64]),
                            ALU.mult)
                    return f
                em.op("pool", mkdec(), ins=[SF, DTR], outs=[SF])
                def mkadd():
                    def f(e):
                        last = None
                        for q in range(4):
                            last = e.tensor_tensor(
                                s_f32[:, q * 512:(q + 1) * 512],
                                s_f32[:, q * 512:(q + 1) * 512],
                                psum[4 + q][:, :], ALU.add)
                        return last
                    return f
                em.op("dve", mkadd(), ins=[SF, PS[4], PS[5], PS[6], PS[7]],
                      outs=[SF])
                em.op("pool", lambda e: e.tensor_copy(s_bf[:, :], s_f32[:, :]),
                      ins=[SF], outs=[SB])
        em.barrier()
        arena.release()

    def xnT_rview(xnTb_r, i):
        return xnTb_r[:, :, i * 128:(i + 1) * 128]

    def phase_mamba2():
        stage = cfg.get("m2_stage", 9)
        arena.mark()
        normw_r = arena.f32(DI)
        dma("sp", normw_r[:, :],
            norm_w.unsqueeze(0).partition_broadcast(128).squeeze(1),
            outs=[B("normw_r")], allow_slow_non_contiguous=True)
        woutT = arena.bf16(16 * D)
        woutT_r = woutT.rearrange("p (k o) -> p k o", k=16)
        prep_rowmajor_T2(W_out, woutT_r, B("woutT"))
        yt = [arena.bf16(DI) for _ in range(2)]
        YT = [B("p2y0"), B("p2y1")]
        zt = [arena.bf16(DI) for _ in range(2)]
        ZT = [B("p2z0"), B("p2z1")]
        x0t = [arena.f32(D) for _ in range(2)]
        X0T = [B("p2x00"), B("p2x01")]
        yg = [arena.f32(DI) for _ in range(2)]
        YG = [B("p2yg0"), B("p2yg1")]
        yn = [arena.bf16(DI) for _ in range(2)]
        YN = [B("p2yn0"), B("p2yn1")]
        ynT = [arena.bf16(16 * 128) for _ in range(2)]
        ynT_r = [t.rearrange("p (k t) -> p k t", k=16) for t in ynT]
        YNT = [B("p2ynT0"), B("p2ynT1")]
        x1t = [arena.f32(D) for _ in range(2)]
        X1T = [B("p2x10"), B("p2x11")]
        h0t = [arena.f32(D) for _ in range(2)]
        H0T = [B("p2h00"), B("p2h01")]
        sc = arena.f32(6 + DI)
        s1, s2, s3, s4, smu, srs = (sc[:, i:i + 1] for i in range(6))
        ssq = sc[:, 6:6 + DI]
        SS = [B(f"p2s{i}") for i in range(6)]

        for b in range(Bc):
            for ti in range(NT):
                t0 = ti * 128
                ybuf, YB = yt[ti % 2], YT[ti % 2]
                zbuf, ZB = zt[ti % 2], ZT[ti % 2]
                xbuf, XB = x0t[ti % 2], X0T[ti % 2]
                x1buf, X1B = x1t[ti % 2], X1T[ti % 2]
                h0buf, H0B = h0t[ti % 2], H0T[ti % 2]
                dma("sp", ybuf[:, :], y_h[b, t0:t0 + 128, :], outs=[YB],
                    ins=[B("y_h")])
                dma("sp", zbuf[:, :], z_h[b, t0:t0 + 128, :], outs=[ZB],
                    ins=[B("z_h")])
                dma("sp", xbuf[:, :], u[b, t0:t0 + 128, :], outs=[XB])
                ygb, YGB = yg[ti % 2], YG[ti % 2]
                ynb, YNB = yn[ti % 2], YN[ti % 2]
                ynTb_r, YNTB = ynT_r[ti % 2], YNT[ti % 2]
                mb0 = 2 * (ti % 2)
                em.op("pool", lambda e, ybuf=ybuf, zbuf=zbuf, ygb=ygb: e.tensor_tensor(
                    ygb[:, :], ybuf[:, :], zbuf[:, :], ALU.mult),
                    ins=[YB, ZB], outs=[YGB])
                if stage < 1:
                    continue
                em.op("act", lambda e, ygb=ygb: e.activation(ssq, ygb[:, :], AF.Square,
                                                             accum_out=s2),
                      ins=[YGB], outs=[SS[1]])
                em.op("act", lambda e: e.activation(srs, s2, AF.Ln, bias=cEPS,
                                                    scale=cIDI),
                      ins=[SS[1]], outs=[SS[5]])
                em.op("act", lambda e: e.activation(srs, srs, AF.Exp, scale=cNHALF),
                      ins=[SS[5]], outs=[SS[5]])
                em.op("dve", lambda e, ynb=ynb, ygb=ygb: e.scalar_tensor_tensor(
                    ynb[:, :], ygb[:, :], srs, normw_r[:], ALU.mult, ALU.mult),
                    ins=[YGB, SS[5], B("normw_r")], outs=[YNB])
                if stage < 2:
                    continue
                transpose_pack(ynb, YNB, 16, ynTb_r, YNTB)
                if stage < 3:
                    continue
                def mkmm(e, ynTb_r=ynTb_r, mb0=mb0):
                    last = None
                    for k in range(16):
                        for s_ in range(2):
                            last = e.matmul(psum[mb0 + s_][:, :],
                                            ynTb_r[:, k, :],
                                            woutT_r[:, k, s_ * 512:(s_ + 1) * 512],
                                            start=(k == 0), stop=(k == 15))
                    return last
                em.op("pe", mkmm, ins=[YNTB, B("woutT")],
                      outs=[PS[mb0], PS[mb0 + 1]])
                if stage < 4:
                    continue
                def mkst(e, mb0=mb0):
                    e.activation(ssq[:, 0:512], psum[mb0][:, :], AF.Copy,
                                 accum_out=s1)
                    return e.activation(ssq[:, 512:1024], psum[mb0 + 1][:, :],
                                        AF.Copy, accum_out=s3)
                em.op("act", mkst, ins=[PS[mb0], PS[mb0 + 1]],
                      outs=[SS[0], SS[2], SS[1]])
                if stage < 5:
                    continue
                em.op("dve", lambda e: e.tensor_tensor(s1, s1, s3, ALU.add),
                      ins=[SS[0], SS[2]], outs=[SS[0]])
                def mksq(e, mb0=mb0):
                    e.activation(ssq[:, 0:512], psum[mb0][:, :], AF.Square,
                                 accum_out=s2)
                    return e.activation(ssq[:, 512:1024], psum[mb0 + 1][:, :],
                                        AF.Square, accum_out=s4)
                em.op("act", mksq, ins=[PS[mb0], PS[mb0 + 1]],
                      outs=[SS[1], SS[3]])
                if stage < 6:
                    continue
                em.op("dve", lambda e: e.tensor_tensor(s2, s2, s4, ALU.add),
                      ins=[SS[1], SS[3]], outs=[SS[1]])
                em.op("dve", lambda e: e.tensor_single_scalar(smu, s1, 1.0 / D,
                                                              ALU.mult),
                      ins=[SS[0]], outs=[SS[4]])
                em.op("dve", lambda e: e.tensor_tensor(srs, smu, smu, ALU.mult),
                      ins=[SS[4]], outs=[SS[5]])
                em.op("dve", lambda e: e.scalar_tensor_tensor(
                    srs, s2, 1.0 / D, srs, ALU.mult, ALU.subtract),
                    ins=[SS[1], SS[5]], outs=[SS[5]])
                em.op("act", lambda e: e.activation(srs, srs, AF.Ln, bias=cEPS),
                      ins=[SS[5]], outs=[SS[5]])
                em.op("act", lambda e: e.activation(srs, srs, AF.Exp, scale=cNHALF),
                      ins=[SS[5]], outs=[SS[5]])
                def mkx1(e, x1buf=x1buf, mb0=mb0):
                    last = None
                    for s_ in range(2):
                        last = e.tensor_scalar(x1buf[:, s_ * 512:(s_ + 1) * 512],
                                               psum[mb0 + s_][:, :], smu, srs,
                                               ALU.subtract, ALU.mult)
                    return last
                em.op("dve", mkx1, ins=[PS[mb0], PS[mb0 + 1], SS[4], SS[5]],
                      outs=[X1B])
                em.op("dve", lambda e, x1buf=x1buf, xbuf=xbuf: e.tensor_tensor(
                    x1buf[:, :], x1buf[:, :], xbuf[:, :], ALU.add),
                    ins=[X1B, XB], outs=[X1B])
                store("sp", x1_h[b, t0:t0 + 128, :], x1buf[:, :], X1B, B("x1_h"))
                ln_tile(x1buf[:, :], h0buf[:, :], X1B, H0B)
                store("sp", hA_h[b, t0:t0 + 128, :], h0buf[:, :], H0B, B("hA_h"))
        em.barrier()
        arena.release()

    def prep_rowmajor_T2(W, wT3, wTbuf):
        arena.mark()
        wtmp = arena.bf16(2 * DI)
        wt = [wtmp[:, 0:DI], wtmp[:, DI:2 * DI]]
        WT = [B("wo0"), B("wo1")]
        for rc in range(8):
            bb, tile = WT[rc % 2], wt[rc % 2]
            dma("pool", tile[:, :], W[rc * 128:(rc + 1) * 128, :], outs=[bb])
            def mkt(tile=tile):
                def f(e):
                    last = None
                    for k in range(16):
                        bank = 6 + k // 8
                        col = (k % 8) * 128
                        last = e.transpose(psb[bank][:, col:col + 128],
                                           tile[:, k * 128:(k + 1) * 128],
                                           ident[:])
                    return last
                return f
            em.op("pe", mkt(), ins=[bb, B("C_IDENT")], outs=[PS[6], PS[7]])
            def mkev(rc=rc):
                def f(e):
                    for half in range(2):
                        e.tensor_copy(
                            wT3[:, 8 * half:8 * half + 8,
                                rc * 128:(rc + 1) * 128],
                            psb[6 + half][:, 0:1024].rearrange(
                                "p (k c) -> p k c", k=8))
                    return e.engine_nop()
                return f
            em.op("dve", mkev(), ins=[PS[6], PS[7]], outs=[wTbuf])
        arena.release()

    def phase_attn(l, hsrc, hdst, HSRC, HDST):
        arena.mark()
        NTL = L // 128
        wqkv = arena.bf16(8 * 3072)
        wqkv_r = wqkv.rearrange("p (k e) -> p k e", k=8)
        WQ = B("wqkv")
        for k in range(8):
            dma("pool", wqkv_r[:, k, :], Wqkv[l, k * 128:(k + 1) * 128, :],
                outs=[WQ], sem="d_wqkv")
        wo = arena.bf16(8 * 1024)
        wo_r = wo.rearrange("p (k e) -> p k e", k=8)
        WOB = B("wob")
        for k in range(8):
            dma("pool", wo_r[:, k, :], Wo[l, k * 128:(k + 1) * 128, :],
                outs=[WOB], sem="d_wob")
        qsrep = arena.f32(64)
        ksrep = arena.f32(64)
        QSC, KSC = B("qsrep"), B("ksrep")
        dma("sp", qsrep[:, :], q_scale[l:l + 1, :].partition_broadcast(128)
            .squeeze(1), outs=[QSC], allow_slow_non_contiguous=True)
        dma("sp", ksrep[:, :], k_scale[l:l + 1, :].partition_broadcast(128)
            .squeeze(1), outs=[KSC], allow_slow_non_contiguous=True)
        base = arena.f32(NTL * 32)
        BASE = B("tblbase")
        tbl = {nm: arena.bf16(NTL * 64) for nm in ("csq", "ssq", "csk", "ssk")}
        TBL = B("tbl")
        for nm, srcn, rep, repbuf in (("csq", "C_CSQ", qsrep, QSC),
                                      ("ssq", "C_SSQ", qsrep, QSC),
                                      ("csk", "C_CSK", ksrep, KSC),
                                      ("ssk", "C_SSK", ksrep, KSC)):
            NH2 = NTL // 2
            for hf in range(2):
                dma("sp", base.rearrange("p (a d) -> p a d", a=NH2),
                    ins[srcn].rearrange("(a p) d -> p a d", p=128)[
                        :, hf * NH2:(hf + 1) * NH2, :],
                    outs=[BASE], sem="d_tblbase")
                dst = tbl[nm].rearrange("p (a d) -> p a d", a=NTL)[
                    :, hf * NH2:(hf + 1) * NH2, :]
                bsr = base.rearrange("p (a d) -> p a d", a=NH2)
                if nm.startswith("cs"):
                    def mk(dst=dst, bsr=bsr, rep=rep):
                        def f(e):
                            return e.tensor_tensor(
                                dst, bsr, rep.unsqueeze(1)
                                .broadcast_to([128, NH2, 64]), ALU.mult)
                        return f
                    em.op("dve", mk(), ins=[BASE, repbuf], outs=[TBL])
                else:
                    def mk1(dst=dst, bsr=bsr, rep=rep):
                        def f(e):
                            return e.tensor_tensor(
                                dst[:, :, 0:32], bsr[:, :, 0:32],
                                rep[:, 32:64].unsqueeze(1)
                                .broadcast_to([128, NH2, 32]), ALU.mult)
                        return f
                    def mk2(dst=dst, bsr=bsr, rep=rep):
                        def f(e):
                            return e.tensor_tensor(
                                dst[:, :, 32:64], bsr[:, :, 32:64],
                                rep[:, 0:32].unsqueeze(1)
                                .broadcast_to([128, NH2, 32]), ALU.mult)
                        return f
                    em.op("dve", mk1(), ins=[BASE, repbuf], outs=[TBL])
                    em.op("dve", mk2(), ins=[BASE, repbuf], outs=[TBL])
        csq_r = tbl["csq"].rearrange("p (a d) -> p a d", a=NTL)
        ssq_r = tbl["ssq"].rearrange("p (a d) -> p a d", a=NTL)
        csk_r = tbl["csk"].rearrange("p (a d) -> p a d", a=NTL)
        ssk_r = tbl["ssk"].rearrange("p (a d) -> p a d", a=NTL)

        kT_ring = arena.bf16(8 * 768)
        kT_r = kT_ring.rearrange("p (c t) -> p c t", c=8)
        KTR = B("kT_ring")
        v_ring = arena.bf16(6 * 1024)
        v_r = v_ring.rearrange("p (s d) -> p s d", s=6)
        VRING = B("v_ring")
        qT = [arena.bf16(8 * 512) for _ in range(2)]
        qT_r = [t.rearrange("p (c t) -> p c t", c=8) for t in qT]
        QT = [B("qT0"), B("qT1")]
        ht = [arena.f32(1024) for _ in range(4)]
        HT = [B(f"a_h{i}") for i in range(4)]
        hsh = [arena.f32(512) for _ in range(4)]
        HSH = [B(f"a_hs{i}") for i in range(4)]
        xs = [arena.bf16(1024) for _ in range(2)]
        XS = [B("a_xs0"), B("a_xs1")]
        xsT = [arena.bf16(8 * 128) for _ in range(2)]
        xsT_r = [t.rearrange("p (c t) -> p c t", c=8) for t in xsT]
        XST = [B("a_xsT0"), B("a_xsT1")]
        qrot = [arena.bf16(1024) for _ in range(2)]
        QR = [B("a_qr0"), B("a_qr1")]
        k1b = [arena.bf16(1024) for _ in range(2)]
        K1B = [B("a_k10"), B("a_k11")]
        krot = [arena.bf16(1024) for _ in range(2)]
        KR = [B("a_kr0"), B("a_kr1")]
        t2b = [arena.bf16(1024) for _ in range(2)]
        T2B = [B("a_t20"), B("a_t21")]
        sqscr = [arena.f32(1024)]
        SQS = [B("a_sq0")]
        rcol = [arena.f32(16) for _ in range(2)]
        RCOL = [B("a_rcol0"), B("a_rcol1")]
        rsq8 = arena.f32(4 * 16)
        rsq8_r = rsq8.rearrange("p (a h) -> p a h", a=4)
        RSQ8 = B("a_rsq8")
        nrsq8 = arena.f32(4 * 16)
        nrsq8_r = nrsq8.rearrange("p (a h) -> p a h", a=4)
        NRSQ8 = B("a_nrsq8")
        rsk = [arena.f32(16) for _ in range(2)]
        RSK = [B("a_rsk0"), B("a_rsk1")]
        scol = [arena.f32(4) for _ in range(2)]
        SCOL = [B("a_scol0"), B("a_scol1")]
        attns = [[arena.bf16(512) for _ in range(2)] for _ in range(2)]
        ATT = [[B("a_at00"), B("a_at01")], [B("a_at10"), B("a_at11")]]
        diag = [arena.bf16(128) for _ in range(2)]
        DIAG = [B("a_diag0"), B("a_diag1")]
        attnT = [arena.bf16(512) for _ in range(2)]
        ATT_T = [B("a_attT0"), B("a_attT1")]
        av = [arena.bf16(8 * 128) for _ in range(2)]
        av_r = [t.rearrange("p (c t) -> p c t", c=8) for t in av]
        AV = [B("a_av0"), B("a_av1")]
        hnew = [arena.f32(1024) for _ in range(2)]
        HN = [B("a_hn0"), B("a_hn1")]

        em.op("pool", lambda e: e.memset(kT_ring[:, :], 0.0), outs=[KTR])
        em.op("pool", lambda e: e.memset(v_ring[:, :], 0.0), outs=[VRING])
        for su in range(2):
            for pr in range(2):
                em.op("pool", lambda e, su=su, pr=pr: e.memset(
                    attns[su][pr][:, :], 0.0), outs=[ATT[su][pr]])

        for b in range(Bc):
            if b > 0:
                for pr in range(2):
                    em.op("pool", lambda e, pr=pr: e.memset(
                        attns[0][pr][:, 0:256], 0.0), outs=[ATT[0][pr]])
                    em.op("pool", lambda e, pr=pr: e.memset(
                        attns[1][pr][:, 128:256], 0.0), outs=[ATT[1][pr]])
            for g in range(NG):
                qTb_r, QTB = qT_r[g % 2], QT[g % 2]
                for i in range(4):
                    t0 = g * 512 + i * 128
                    hb, HB = ht[i], HT[i]
                    dma("sp", hb[:, :], hsrc[b, t0:t0 + 128, :], outs=[HB],
                        ins=[HSRC])
                    if t0 == 0:
                        em.op("dve", lambda e, i=i: e.memset(hsh[i][0:1, :], 0.0),
                              outs=[HSH[i]])
                        dma("sp", hsh[i][1:128, :], hsrc[b, 0:127, 512:1024],
                            outs=[HSH[i]], ins=[HSRC])
                    else:
                        dma("sp", hsh[i][:, :],
                            hsrc[b, t0 - 1:t0 + 127, 512:1024],
                            outs=[HSH[i]], ins=[HSRC])
                for i in range(4):
                    gt = g * 4 + i
                    par = i % 2
                    xsb, XSB = xs[par], XS[par]
                    xsT_rb, XSTB = xsT_r[par], XST[par]
                    qrb, QRB = qrot[par], QR[par]
                    k1bb, K1BB = k1b[par], K1B[par]
                    krb, KRB = krot[par], KR[par]
                    t2bb, T2BB = t2b[par], T2B[par]
                    sqb, SQSB = sqscr[0], SQS[0]
                    rcb, RCOLB = rcol[par], RCOL[par]
                    rskb, RSKB = rsk[par], RSK[par]
                    hb, HB = ht[i], HT[i]
                    em.op("dve", lambda e, hb=hb, xsb=xsb: e.tensor_copy(
                        xsb[:, 0:512], hb[:, 0:512]), ins=[HB], outs=[XSB])
                    em.op("dve", lambda e, i=i, xsb=xsb: e.tensor_copy(
                        xsb[:, 512:1024], hsh[i][:, :]), ins=[HSH[i]], outs=[XSB])
                    transpose_pack(xsb, XSB, 8, xsT_rb, XSTB)
                    def mkqkv(e, xsT_rb=xsT_rb):
                        last = None
                        for k in range(8):
                            st = xsT_rb[:, k, :]
                            for s in range(6):
                                last = e.matmul(
                                    psum[s][:, :], st,
                                    wqkv_r[:, k, s * 512:(s + 1) * 512],
                                    start=(k == 0), stop=(k == 7))
                        return last
                    em.op("pe", mkqkv, ins=[XSTB, WQ],
                          outs=[PS[0], PS[1], PS[2], PS[3], PS[4], PS[5]])
                    def mkv(e, gt=gt):
                        e.tensor_copy(v_r[:, gt % 6, 0:512], psum[4][:, :])
                        return e.tensor_copy(v_r[:, gt % 6, 512:1024],
                                             psum[5][:, :])
                    em.op("dve", mkv, ins=[PS[4], PS[5]], outs=[VRING])
                    def mkq2(e, sqb=sqb):
                        e.activation(sqb[:, 0:512], psum[0][:, :], AF.Square)
                        return e.activation(sqb[:, 512:1024], psum[1][:, :],
                                            AF.Square)
                    em.op("act", mkq2, ins=[PS[0], PS[1]], outs=[SQSB])
                    em.op("dve", lambda e, rcb=rcb, sqb=sqb: e.tensor_reduce(
                        rcb[:, :], sqb.rearrange("p (h d) -> p h d", h=16),
                        AX.X, ALU.add), ins=[SQSB], outs=[RCOLB])
                    em.op("act", lambda e, rcb=rcb: e.activation(
                        rcb[:, :], rcb[:, :], AF.Sqrt),
                        ins=[RCOLB], outs=[RCOLB])
                    em.op("dve", lambda e, rcb=rcb: e.tensor_single_scalar(
                        rcb[:, :], rcb[:, :], 1e-12, ALU.max),
                        ins=[RCOLB], outs=[RCOLB])
                    em.op("dve", lambda e, rcb=rcb: e.reciprocal(
                        rcb[:, :], rcb[:, :]), ins=[RCOLB], outs=[RCOLB])
                    em.op("dve", lambda e, i=i, rcb=rcb: e.tensor_single_scalar(
                        rsq8_r[:, i, :], rcb[:, :], 8.0, ALU.mult),
                        ins=[RCOLB], outs=[RSQ8])
                    em.op("dve", lambda e, i=i, rcb=rcb: e.tensor_single_scalar(
                        nrsq8_r[:, i, :], rcb[:, :], -8.0, ALU.mult),
                        ins=[RCOLB], outs=[NRSQ8])
                    def mkk2(e, sqb=sqb):
                        e.activation(sqb[:, 0:512], psum[2][:, :], AF.Square)
                        return e.activation(sqb[:, 512:1024], psum[3][:, :],
                                            AF.Square)
                    em.op("act", mkk2, ins=[PS[2], PS[3]], outs=[SQSB])
                    em.op("dve", lambda e, rskb=rskb, sqb=sqb: e.tensor_reduce(
                        rskb[:, :], sqb.rearrange("p (h d) -> p h d", h=16),
                        AX.X, ALU.add), ins=[SQSB], outs=[RSKB])
                    em.op("act", lambda e, rskb=rskb: e.activation(
                        rskb[:, :], rskb[:, :], AF.Sqrt),
                        ins=[RSKB], outs=[RSKB])
                    em.op("dve", lambda e, rskb=rskb: e.tensor_single_scalar(
                        rskb[:, :], rskb[:, :], 1e-12, ALU.max),
                        ins=[RSKB], outs=[RSKB])
                    em.op("dve", lambda e, rskb=rskb: e.reciprocal(
                        rskb[:, :], rskb[:, :]), ins=[RSKB], outs=[RSKB])
                    def mkkh(e, k1bb=k1bb, rskb=rskb):
                        for s in range(2):
                            e.tensor_tensor(
                                k1bb[:, s * 512:(s + 1) * 512].rearrange(
                                    "p (h d) -> p h d", h=8),
                                psum[2 + s][:, :].rearrange(
                                    "p (h d) -> p h d", h=8),
                                rskb[:, s * 8:(s + 1) * 8].unsqueeze(2)
                                .broadcast_to([128, 8, 64]), ALU.mult)
                        return e.engine_nop()
                    em.op("dve", mkkh, ins=[PS[2], PS[3], RSKB], outs=[K1BB])
                    def mkkr1(e, gt=gt, krb=krb, k1bb=k1bb):
                        return e.tensor_tensor(
                            krb.rearrange("p (h d) -> p h d", h=16),
                            k1bb.rearrange("p (h d) -> p h d", h=16),
                            csk_r[:, gt, :].unsqueeze(1)
                            .broadcast_to([128, 16, 64]), ALU.mult)
                    em.op("pool", mkkr1, ins=[K1BB, TBL], outs=[KRB])
                    def mkkr2(e, gt=gt, t2bb=t2bb, k1bb=k1bb):
                        k3 = k1bb.rearrange("p (h d) -> p h d", h=16)
                        t3 = t2bb.rearrange("p (h d) -> p h d", h=16)
                        e.tensor_tensor(t3[:, :, 0:32], k3[:, :, 32:64],
                                        ssk_r[:, gt, 0:32].unsqueeze(1)
                                        .broadcast_to([128, 16, 32]), ALU.mult)
                        return e.tensor_tensor(
                            t3[:, :, 32:64], k3[:, :, 0:32],
                            ssk_r[:, gt, 32:64].unsqueeze(1)
                            .broadcast_to([128, 16, 32]), ALU.mult)
                    em.op("pool", mkkr2, ins=[K1BB, TBL], outs=[T2BB])
                    em.op("pool", lambda e, krb=krb, t2bb=t2bb: e.tensor_tensor(
                        krb[:, :], krb[:, :], t2bb[:, :], ALU.add),
                        ins=[KRB, T2BB], outs=[KRB])
                    def mkqr1(e, gt=gt, qrb=qrb):
                        for s in range(2):
                            e.tensor_tensor(
                                qrb[:, s * 512:(s + 1) * 512].rearrange(
                                    "p (h d) -> p h d", h=8),
                                psum[s][:, :].rearrange("p (h d) -> p h d", h=8),
                                csq_r[:, gt, :].unsqueeze(1)
                                .broadcast_to([128, 8, 64]), ALU.mult)
                        return e.engine_nop()
                    em.op("dve", mkqr1, ins=[PS[0], PS[1], TBL], outs=[QRB])
                    def mkqr2(e, gt=gt, t2bb=t2bb):
                        t3 = t2bb.rearrange("p (h d) -> p h d", h=16)
                        for s in range(2):
                            q3 = psum[s][:, :].rearrange("p (h d) -> p h d", h=8)
                            e.tensor_tensor(t3[:, s * 8:(s + 1) * 8, 0:32],
                                            q3[:, :, 32:64],
                                            ssq_r[:, gt, 0:32].unsqueeze(1)
                                            .broadcast_to([128, 8, 32]),
                                            ALU.mult)
                            e.tensor_tensor(t3[:, s * 8:(s + 1) * 8, 32:64],
                                            q3[:, :, 0:32],
                                            ssq_r[:, gt, 32:64].unsqueeze(1)
                                            .broadcast_to([128, 8, 32]),
                                            ALU.mult)
                        return e.engine_nop()
                    em.op("dve", mkqr2, ins=[PS[0], PS[1], TBL], outs=[T2BB])
                    em.op("dve", lambda e, qrb=qrb, t2bb=t2bb: e.tensor_tensor(
                        qrb[:, :], qrb[:, :], t2bb[:, :], ALU.add),
                        ins=[QRB, T2BB], outs=[QRB])
                    if cfg.get("attn_taps") and b == 0:
                        store("sp", qkvtap_h[0, gt, :, :], qrb[:, :], QRB,
                              B("qkvtap"))
                        store("sp", qkvtap_h[1, gt, :, :], krb[:, :], KRB,
                              B("qkvtap"))
                        def mkvt(e, gt=gt):
                            return e.dma_start(out=qkvtap_h[2, gt, :, :],
                                               in_=v_r[:, gt % 6, :])
                        em.op("sp", mkvt, ins=[VRING], outs=[B("qkvtap")],
                              dma=True, sem="d_st_vtap")
                    transpose_pack(qrb, QRB, 8,
                                   qTb_r[:, :, i * 128:(i + 1) * 128], QTB)
                    transpose_pack(krb, KRB, 8,
                                   kT_r[:, :, 256 + i * 128:256 + (i + 1) * 128],
                                   KTR)
                for wloc in range(2):
                    w = 2 * g + wloc
                    keyoff = wloc * 256
                    for sub in range(2):
                        i = wloc * 2 + sub
                        gt = g * 4 + i
                        avb, av_rb, AVB = av[sub], av_r[sub], AV[sub]
                        if w > 0:
                            vs, ve = sub * 128, 384 + sub * 128
                            masks = [(tril_klt, B("C_TRIL_KLT"), sub * 128),
                                     (triu_kgt, B("C_TRIU_KGT"),
                                      256 + sub * 128)]
                        else:
                            vs, ve = 256, 384 + sub * 128
                            masks = [(triu_kgt, B("C_TRIU_KGT"),
                                      256 + sub * 128)]
                        for hp in range(8):
                            par = hp % 2
                            att, ATTB = attns[sub][par], ATT[sub][par]
                            sc2, SC2B = scol[par], SCOL[par]
                            dg, DGB = diag[par], DIAG[par]
                            atT, ATT_TB = attnT[par], ATT_T[par]
                            sb0 = 2 * par
                            def mksc(e, hp=hp, vs=vs, ve=ve, masks=masks, sb0=sb0,
                                     sub=sub, wloc=wloc, keyoff=keyoff,
                                     qTb_r=qTb_r):
                                last = None
                                for hh in range(2):
                                    r0, r1 = hh * 64, hh * 64 + 64
                                    e.matmul(
                                        psum[sb0 + hh][:, vs:ve],
                                        qTb_r[r0:r1, hp,
                                              wloc * 256 + sub * 128:
                                              wloc * 256 + sub * 128 + 128],
                                        kT_r[r0:r1, hp,
                                             keyoff + vs:keyoff + ve],
                                        start=True, stop=False)
                                    for mi, (mt, mb, mc) in enumerate(masks):
                                        last = e.matmul(
                                            psum[sb0 + hh][:, mc:mc + 128],
                                            mt[:, :], negeye[:, :],
                                            start=False,
                                            stop=(mi == len(masks) - 1))
                                return last
                            def _sc_outs(sb0=sb0):
                                return [PS[sb0], PS[sb0 + 1]]
                            em.op("pe", mksc,
                                  ins=[QTB, KTR, B("C_TRIL_KLT"),
                                       B("C_TRIU_KGT"), B("C_NEGEYE")],
                                  outs=_sc_outs())
                            for hh in range(2):
                                h = 2 * hp + hh
                                def mkcp(e, hh=hh, vs=vs, ve=ve, att=att,
                                         sb0=sb0):
                                    return e.activation(att[:, vs:ve],
                                                        psum[sb0 + hh][:, vs:ve],
                                                        AF.Copy)
                                em.op("act", mkcp, ins=[PS[sb0 + hh]],
                                      outs=[ATTB])
                                if (cfg.get("attn_taps") and b == 0 and w == 1
                                        and sub == 0 and hp == 0 and hh == 0):
                                    def mkrt(e, att=att):
                                        return e.dma_start(out=attap_h[2, :, :],
                                                           in_=att[:, :])
                                    em.op("sp", mkrt, ins=[ATTB],
                                          outs=[B("attap")], dma=True,
                                          sem="d_st_attap")
                                def mkmax(e, vs=vs, ve=ve, att=att, sc2=sc2):
                                    return e.tensor_reduce(
                                        sc2[:, 0:1], att[:, vs:ve], AX.X,
                                        ALU.max)
                                em.op("dve", mkmax, ins=[ATTB], outs=[SC2B])
                                def mkbias(e, i=i, h=h, sc2=sc2):
                                    return e.tensor_tensor(
                                        sc2[:, 1:2], sc2[:, 0:1],
                                        nrsq8_r[:, i, h:h + 1], ALU.mult)
                                em.op("dve", mkbias, ins=[SC2B, NRSQ8],
                                      outs=[SC2B])
                                def mkexp(e, hh=hh, vs=vs, ve=ve, att=att,
                                          i=i, h=h, sc2=sc2, sb0=sb0):
                                    return e.activation(
                                        att[:, vs:ve], psum[sb0 + hh][:, vs:ve],
                                        AF.Exp, bias=sc2[:, 1:2],
                                        scale=rsq8_r[:, i, h:h + 1],
                                        accum_out=sc2[:, 2:3])
                                em.op("act", mkexp,
                                      ins=[PS[sb0 + hh], SC2B, RSQ8],
                                      outs=[ATTB, SC2B])
                                em.op("dve", lambda e, sc2=sc2: e.reciprocal(
                                    sc2[:, 3:4], sc2[:, 2:3]),
                                    ins=[SC2B], outs=[SC2B])
                                em.op("dve", lambda e, dg=dg, sc2=sc2:
                                      e.tensor_scalar(
                                          dg[:, :], ident[:], sc2[:, 3:4],
                                          None, ALU.mult),
                                      ins=[SC2B, B("C_IDENT")], outs=[DGB])
                                def mkat(e, att=att, dg=dg, par=par):
                                    last = None
                                    for kc in range(4):
                                        last = e.matmul(
                                            psum[4 + par][:, kc * 128:(kc + 1) * 128],
                                            att[:, kc * 128:(kc + 1) * 128],
                                            dg[:, :], start=True, stop=True)
                                    return last
                                em.op("pe", mkat, ins=[ATTB, DGB],
                                      outs=[PS[4 + par]])
                                em.op("dve", lambda e, atT=atT, par=par:
                                      e.tensor_copy(atT[:, :],
                                                    psum[4 + par][:, :]),
                                      ins=[PS[4 + par]], outs=[ATT_TB])
                                if (cfg.get("attn_taps") and b == 0
                                        and w == 1 and sub == 0 and hp == 0
                                        and hh == 0):
                                    def mkat1(e, att=att):
                                        return e.dma_start(
                                            out=attap_h[0, :, :],
                                            in_=att[:, :])
                                    em.op("sp", mkat1, ins=[ATTB],
                                          outs=[B("attap")], dma=True,
                                          sem="d_st_attap")
                                    def mkat2(e):
                                        return e.dma_start(
                                            out=attap_h[1, :, :],
                                            in_=attnT[:, :])
                                    em.op("sp", mkat2, ins=[ATT_T],
                                          outs=[B("attap")], dma=True,
                                          sem="d_st_attap")
                                def mkav(e, w=w, h=h, hh=hh, atT=atT,
                                         par=par):
                                    last = None
                                    for kc in range(4):
                                        slot = (2 * w - 2 + kc) % 6
                                        last = e.matmul(
                                            psum[6 + par][0:64,
                                                    hh * 128:hh * 128 + 128],
                                            v_r[:, slot, h * 64:(h + 1) * 64],
                                            atT[:, kc * 128:(kc + 1) * 128],
                                            start=(kc == 0), stop=(kc == 3))
                                    return last
                                em.op("pe", mkav, ins=[ATT_TB, VRING],
                                      outs=[PS[6 + par]])
                            def mkave(e, hp=hp, av_rb=av_rb, par=par):
                                e.tensor_copy(av_rb[0:64, hp, :],
                                              psum[6 + par][0:64, 0:128])
                                return e.tensor_copy(av_rb[64:128, hp, :],
                                                     psum[6 + par][0:64,
                                                                   128:256])
                            em.op("dve", mkave, ins=[PS[6 + par]], outs=[AVB])
                        if cfg.get("attn_taps") and b == 0:
                            def mkavt(e, w=w, sub=sub, avb=avb):
                                return e.dma_start(
                                    out=avtap_h[w * 2 + sub, :, :],
                                    in_=avb[:, :])
                            em.op("sp", mkavt, ins=[AVB], outs=[B("avtap")],
                                  dma=True, sem="d_st_avtap")
                        t0 = w * 256 + sub * 128
                        hb, HB = ht[i], HT[i]
                        hn, HNB = hnew[sub], HN[sub]
                        wb0 = 2 * sub
                        def mkwo(e, av_rb=av_rb, wb0=wb0):
                            last = None
                            for k in range(8):
                                for s in range(2):
                                    last = e.matmul(
                                        psum[wb0 + s][:, :], av_rb[:, k, :],
                                        wo_r[:, k, s * 512:(s + 1) * 512],
                                        start=(k == 0), stop=(k == 7))
                            return last
                        em.op("pe", mkwo, ins=[AVB, WOB],
                              outs=[PS[wb0], PS[wb0 + 1]])
                        def mkres(e, hn=hn, hb=hb, wb0=wb0):
                            for s in range(2):
                                e.tensor_tensor(hn[:, s * 512:(s + 1) * 512],
                                                psum[wb0 + s][:, :],
                                                hb[:, s * 512:(s + 1) * 512],
                                                ALU.add)
                            return e.engine_nop()
                        em.op("dve", mkres, ins=[PS[wb0], PS[wb0 + 1], HB],
                              outs=[HNB])
                        store("sp", hdst[b, t0:t0 + 128, :], hn[:, :], HNB,
                              HDST)
                em.op("pool", lambda e: e.tensor_copy(
                    kT_r[:, :, 0:256], kT_r[:, :, 512:768]),
                    ins=[KTR], outs=[KTR])
        em.barrier()
        arena.release()

    def phase_ffnA(l, hsrc, HSRC):
        arena.mark()
        w1 = arena.bf16(8 * 2 * DFF)
        w1_r = w1.rearrange("p (k e) -> p k e", k=8)
        W1B = B("w1b")
        for k in range(8):
            dma("pool", w1_r[:, k, :], W1[l, k * 128:(k + 1) * 128, :],
                outs=[W1B], sem="d_w1b")
        gcol = arena.f32(8)
        GCOL = B("f_gcol")
        dma("sp", gcol[:, :], ln_g[l].rearrange("(c p) -> p c", p=128),
            outs=[GCOL], allow_slow_non_contiguous=True)
        bcol = arena.bf16(8)
        BCOL = B("f_bcol")
        dma("pool", bcol[:, :], ln_b[l].rearrange("(c p) -> p c", p=128),
            outs=[BCOL], allow_slow_non_contiguous=True)
        b1row = arena.f32(2 * DFF)
        B1R = B("f_b1row")
        dma("sp", b1row[0:1, :], b1[l:l + 1, :], outs=[B1R])
        brow_sb = arena.f32(2 * DFF)
        BRS = B("f_brs")
        for j in range(11):
            n0, n1 = j * 512, min(j * 512 + 512, 2 * DFF)
            def mkb(e, j=j, n0=n0, n1=n1):
                last = None
                for k in range(8):
                    last = e.matmul(psum[j % 2][0:1, 0:n1 - n0],
                                    bcol[:, k:k + 1], w1_r[:, k, n0:n1],
                                    start=(k == 0), stop=(k == 7))
                return last
            em.op("pe", mkb, ins=[BCOL, W1B], outs=[PS[j % 2]])
            def mkbe(e, j=j, n0=n0, n1=n1):
                return e.tensor_copy(brow_sb[0:1, n0:n1],
                                     psum[j % 2][0:1, 0:n1 - n0])
            em.op("dve", mkbe, ins=[PS[j % 2]], outs=[BRS])
        em.op("dve", lambda e: e.tensor_tensor(
            brow_sb[0:1, :], brow_sb[0:1, :], b1row[0:1, :], ALU.add),
            ins=[BRS, B1R], outs=[BRS])
        store("sp", brow_h[0:2 * DFF].unsqueeze(0), brow_sb[0:1, :],
              BRS, B("brow_h"))
        b1c = arena.f32(44)
        B1C = B("f_b1c")
        dma("sp", b1c[:, 0:22],
            brow_h[0:2816].rearrange("(c p) -> p c", p=128),
            outs=[B1C], ins=[B("brow_h")], allow_slow_non_contiguous=True)
        dma("sp", b1c[:, 22:44],
            brow_h[DFF:DFF + 2816].rearrange("(c p) -> p c", p=128),
            outs=[B1C], ins=[B("brow_h")], allow_slow_non_contiguous=True)
        for k in range(8):
            def mkg(e, k=k):
                return e.tensor_scalar(w1_r[:, k, :], w1_r[:, k, :],
                                       gcol[:, k:k + 1], None, ALU.mult)
            em.op("pool", mkg, ins=[W1B, GCOL], outs=[W1B])

        ht = [arena.f32(1024) for _ in range(2)]
        HT2 = [B("f_h0"), B("f_h1")]
        hsh = [arena.f32(512) for _ in range(2)]
        HSH2 = [B("f_hs0"), B("f_hs1")]
        xn = [arena.bf16(1024) for _ in range(2)]
        XN2 = [B("f_xn0"), B("f_xn1")]
        xnT = [arena.bf16(8 * 512) for _ in range(2)]
        xnT_r = [t.rearrange("p (k t) -> p k t", k=8) for t in xnT]
        XNT2 = [B("f_xnT0"), B("f_xnT1")]
        tg = [arena.bf16(512) for _ in range(2)]
        TG = [B("f_tg0"), B("f_tg1")]
        shtile = [arena.f32(1024) for _ in range(2)]
        SHT = [B("f_sht0"), B("f_sht1")]
        gact = [arena.bf16(512) for _ in range(2)]
        GACT = [B("f_g0"), B("f_g1")]

        for b in range(Bc):
            for g in range(NG):
                xnTb_r, XNTB = xnT_r[g % 2], XNT2[g % 2]
                for i in range(4):
                    t0 = g * 512 + i * 128
                    hb, HB = ht[i % 2], HT2[i % 2]
                    hs, HS = hsh[i % 2], HSH2[i % 2]
                    dma("sp", hb[:, :], hsrc[b, t0:t0 + 128, :], outs=[HB],
                        ins=[HSRC])
                    if t0 == 0:
                        em.op("dve", lambda e, hs=hs: e.memset(hs[0:1, :], 0.0),
                              outs=[HS])
                        dma("sp", hs[1:128, :], hsrc[b, 0:127, 512:1024],
                            outs=[HS], ins=[HSRC])
                    else:
                        dma("sp", hs[:, :], hsrc[b, t0 - 1:t0 + 127, 512:1024],
                            outs=[HS], ins=[HSRC])
                    shb, SHB = shtile[i % 2], SHT[i % 2]
                    xnb2, XNB2 = xn[i % 2], XN2[i % 2]
                    em.op("dve", lambda e, hb=hb, shb=shb: e.tensor_copy(
                        shb[:, 0:512], hb[:, 0:512]), ins=[HB], outs=[SHB])
                    em.op("dve", lambda e, hs=hs, shb=shb: e.tensor_copy(
                        shb[:, 512:1024], hs[:, :]), ins=[HS], outs=[SHB])
                    ln_tile(shb[:, :], xnb2[:, :], SHB, XNB2)
                    transpose_pack(xnb2, XNB2, 8,
                                   xnTb_r[:, :, i * 128:(i + 1) * 128], XNT2[g % 2])
                for j in range(22):
                    a0 = j * 128
                    nr = min(128, DFF - a0)
                    g0 = DFF + a0
                    bkA, bkG = 2 * (j % 2), 2 * (j % 2) + 1
                    def mkw1(e, a0=a0, g0=g0, nr=nr, bkA=bkA, bkG=bkG,
                             xnTb_r=xnTb_r):
                        last = None
                        for k in range(8):
                            e.matmul(psum[bkA][0:nr, :],
                                     w1_r[:, k, a0:a0 + nr], xnTb_r[:, k, :],
                                     start=(k == 0), stop=(k == 7))
                            last = e.matmul(psum[bkG][0:nr, :],
                                            w1_r[:, k, g0:g0 + nr],
                                            xnTb_r[:, k, :],
                                            start=(k == 0), stop=(k == 7))
                        return last
                    em.op("pe", mkw1, ins=[XNT2[g % 2], W1B],
                          outs=[PS[bkA], PS[bkG]])
                    tgb, TGB = tg[j % 2], TG[j % 2]
                    def mkgel2(e, j=j, nr=nr, bkG=bkG, tgb=tgb):
                        return e.activation(tgb[0:nr, :], psum[bkG][0:nr, :],
                                            AF.Gelu,
                                            bias=b1c[0:nr, 22 + j:23 + j])
                    em.op("act", mkgel2, ins=[PS[bkG], B1C], outs=[TGB])
                    gb, GB = gact[j % 2], GACT[j % 2]
                    def mkga(e, j=j, nr=nr, bkA=bkA, gb=gb, tgb=tgb):
                        return e.scalar_tensor_tensor(
                            gb[0:nr, :], psum[bkA][0:nr, :],
                            b1c[0:nr, j:j + 1], tgb[0:nr, :], ALU.add, ALU.mult)
                    em.op("dve", mkga, ins=[PS[bkA], B1C, TGB], outs=[GB])
                    def mkgs(e, j=j, nr=nr, b=b, g=g, gb=gb):
                        return e.dma_start(
                            out=g_h[b, j * 128:j * 128 + nr,
                                    g * 512:(g + 1) * 512],
                            in_=gb[0:nr, :])
                    em.op("sp", mkgs, ins=[GB], outs=[B("g_h")], dma=True,
                          sem="d_st_" + GB.name, waw=False)
        em.barrier()
        arena.release()

    def phase_ffnB(l, hsrc, hdst, HSRC, HDST, final=False):
        arena.mark()
        w2 = arena.bf16(22 * 1024)
        w2_r = w2.rearrange("p (k e) -> p k e", k=22)
        W2B = B("w2b")
        for k in range(22):
            r0, r1 = k * 128, min(k * 128 + 128, DFF)
            dma("pool", w2_r[0:r1 - r0, k, :], W2[l, r0:r1, :], outs=[W2B],
                sem="d_w2b")
        b2row = arena.bf16(1024)
        B2R = B("f_b2row")
        dma("pool", b2row[0:1, :], b2[l:l + 1, :], outs=[B2R], sem="d_b2row")
        gsb = [arena.bf16(22 * 128) for _ in range(2)]
        gsb_r = [t.rearrange("p (k t) -> p k t", k=22) for t in gsb]
        GSB = [B("f2_g0"), B("f2_g1")]
        ht = [arena.f32(1024) for _ in range(2)]
        HT3 = [B("f2_h0"), B("f2_h1")]
        hn = [arena.f32(1024) for _ in range(2)]
        HN3 = [B("f2_hn0"), B("f2_hn1")]
        x1t = [arena.f32(1024) for _ in range(2)]
        X1T3 = [B("f2_x10"), B("f2_x11")]

        for b in range(Bc):
            for ti in range(NT):
                t0 = ti * 128
                gb_r, GB = gsb_r[ti % 2], GSB[ti % 2]
                hb, HB = ht[ti % 2], HT3[ti % 2]
                hnb, HNB = hn[ti % 2], HN3[ti % 2]
                for k in range(22):
                    r0, r1 = k * 128, min(k * 128 + 128, DFF)
                    def mkgl(e, k=k, r0=r0, r1=r1, b=b, t0=t0, gb_r=gb_r):
                        return e.dma_start(out=gb_r[0:r1 - r0, k, :],
                                           in_=g_h[b, r0:r1, t0:t0 + 128])
                    em.op("sp", mkgl, ins=[B("g_h")], outs=[GB], dma=True)
                dma("sp", hb[:, :], hsrc[b, t0:t0 + 128, :], outs=[HB],
                    ins=[HSRC])
                fb0 = 2 * (ti % 2)
                def mkw2(e, gb_r=gb_r, fb0=fb0):
                    last = None
                    for k in range(22):
                        r1 = min(k * 128 + 128, DFF) - k * 128
                        for s in range(2):
                            e.matmul(psum[fb0 + s][:, :], gb_r[0:r1, k, :],
                                     w2_r[0:r1, k, s * 512:(s + 1) * 512],
                                     start=(k == 0), stop=False)
                    for s in range(2):
                        last = e.matmul(psum[fb0 + s][:, :], ones1[:],
                                        b2row[0:1, s * 512:(s + 1) * 512],
                                        start=False, stop=True)
                    return last
                em.op("pe", mkw2, ins=[GB, W2B, B2R, B("C_ONES1")],
                      outs=[PS[fb0], PS[fb0 + 1]])
                def mkr(e, hnb=hnb, hb=hb, fb0=fb0):
                    for s in range(2):
                        e.tensor_tensor(hnb[:, s * 512:(s + 1) * 512],
                                        psum[fb0 + s][:, :],
                                        hb[:, s * 512:(s + 1) * 512], ALU.add)
                    return e.engine_nop()
                em.op("dve", mkr, ins=[PS[fb0], PS[fb0 + 1], HB], outs=[HNB])
                if not final:
                    store("sp", hdst[b, t0:t0 + 128, :], hnb[:, :], HNB, HDST)
                else:
                    x1b, X1B = x1t[ti % 2], X1T3[ti % 2]
                    dma("sp", x1b[:, :], x1_h[b, t0:t0 + 128, :], outs=[X1B],
                        ins=[B("x1_h")])
                    ln_tile(hnb[:, :], hnb[:, :], HNB, HNB)
                    em.op("dve", lambda e, hnb=hnb, x1b=x1b: e.tensor_tensor(
                        hnb[:, :], hnb[:, :], x1b[:, :], ALU.add),
                        ins=[HNB, X1B], outs=[HNB])
                    store("sp", OUT[b, t0:t0 + 128, :], hnb[:, :], HNB,
                          B("OUT"))
        em.barrier()
        arena.release()

    phases = cfg.get("phases", ("m1", "m2"))
    if "m1" in phases:
        phase_m1a()
        phase_m1b()
    if "m2" in phases:
        phase_mamba2()
    if "tf" in phases:
        hsrcs = [(hA_h, B("hA_h")), (hB_h, B("hB_h"))]
        for l in range(depth):
            a, bb = hsrcs[0], hsrcs[1]
            phase_attn(l, a[0], bb[0], a[1], bb[1])
            phase_ffnA(l, bb[0], bb[1])
            phase_ffnB(l, bb[0], a[0], bb[1], a[1], final=(l == depth - 1))

    for name, (dst, srcap) in dbg.items():
        shape = srcap.tensor.shape
        if len(shape) == 3:
            for b in range(shape[0]):
                for r0 in range(0, shape[1], 512):
                    r1 = min(r0 + 512, shape[1])
                    dma("sp", dst[b, r0:r1, :], srcap[b, r0:r1, :],
                        sem="d_tap")
        else:
            for r0 in range(0, shape[0], 512):
                r1 = min(r0 + 512, shape[0])
                dma("sp", dst[r0:r1, :], srcap[r0:r1, :], sem="d_tap")
    em.barrier()

    import contextlib
    with contextlib.ExitStack() as stack:
        semnames = sorted(em.done.keys())
        assert len(semnames) <= 200, f"too many semaphores: {len(semnames)}"
        sems = {s: stack.enter_context(nc.semaphore("sem_" + s))
                for s in semnames}
        block = stack.enter_context(nc.Block())

        def runner(ename):
            def run(e):
                for waits, fn, s, inc in em.cmds[ename]:
                    for ds, dv in waits.items():
                        e.wait_ge(sems[ds], dv)
                    if fn is not None:
                        last = fn(e)
                        if inc:
                            last.then_inc(sems[s], inc)
            return run
        block.tensor(runner("pe"))
        block.vector(runner("dve"))
        block.scalar(runner("act"))
        block.gpsimd(runner("pool"))
        block.sync(runner("sp"))

    class P:
        pass
    p = P()
    p.nc = nc
    p.const_arrays = consts
    p.em = em
    return p


def kernel(**inputs):
    cfg = {"L": 4096, "B": 2, "depth": 6, "phases": ("m1", "m2", "tf"),
           "debug": ()}
    p = build_program(cfg)
    n = 8
    in_maps = []
    for c in range(n):
        m = {}
        for k, v in inputs.items():
            if k == "L0_attn_u":
                m[k] = np.ascontiguousarray(v[2 * c:2 * c + 2])
            else:
                m[k] = np.ascontiguousarray(v)
        for k, v in p.const_arrays.items():
            m[k] = v
        in_maps.append(m)
    res = run_bass_kernel_spmd(p.nc, in_maps, core_ids=list(range(n)))
    return np.concatenate([r["OUT"] for r in res.results], axis=0)


# version 05 (final)
# speedup vs baseline: 1.1602x; 1.1602x over previous
import numpy as np
import ml_dtypes
import concourse.bass as bass
import concourse.mybir as mybir
from concourse.bass_utils import run_bass_kernel_spmd

F32 = mybir.dt.float32
BF16 = mybir.dt.bfloat16
AF = mybir.ActivationFunctionType
ALU = mybir.AluOpType
AX = mybir.AxisListType
BF = ml_dtypes.bfloat16

D = 1024
DI = 2048
DST = 128
NH = 32
HD = 64
CONVD = DI + 2 * DST
DPROJ = 2 * DI + 2 * DST + NH
AH = 16
AHD = 64
WIN = 256
DFF = 2730
NEG = -1.0e38

ENGINES = ("pe", "dve", "act", "pool", "sp")


class Buf:
    __slots__ = ("name", "w", "readers")

    def __init__(self, name):
        self.name = name
        self.w = None
        self.readers = {}


class Em:

    def __init__(self):
        self.cmds = {e: [] for e in ENGINES}
        self.done = {}
        self.synced = {e: {} for e in ENGINES}
        self.nops = 0
        for e in ENGINES:
            self.done[e] = 0

    sem_alias = {}

    def op(self, eng, fn, ins=(), outs=(), dma=False, sem=None, waw=True):
        if dma:
            s = sem or ("d_" + (outs[0].name if outs else "null"))
            s = self.sem_alias.get(s, s)
        else:
            s = eng
        inc = 16 if dma else 1
        waits = {}
        syn = self.synced[eng]

        def need(dep):
            if dep is None:
                return
            ds, dv = dep
            if dv <= syn.get(ds, 0):
                return
            if dv > waits.get(ds, 0):
                waits[ds] = dv

        for b in ins:
            need(b.w)
        for b in outs:
            if waw:
                need(b.w)
            for ds, dv in b.readers.items():
                need((ds, dv))
        for ds, dv in waits.items():
            syn[ds] = dv
        self.done[s] = self.done.get(s, 0) + inc
        val = self.done[s]
        self.cmds[eng].append((waits, fn, s, inc))
        self.nops += 1
        for b in outs:
            b.w = (s, val)
            b.readers = {}
        for b in ins:
            if b.readers.get(s if False else eng, 0) < val and not dma:
                b.readers[eng] = val
            elif dma:
                if b.readers.get(s, 0) < val:
                    b.readers[s] = val

    def barrier(self):
        for e in ENGINES:
            waits = {}
            syn = self.synced[e]
            for s, dv in self.done.items():
                if s != e and dv > syn.get(s, 0):
                    waits[s] = dv
                    syn[s] = dv
            if waits:
                self.cmds[e].append((waits, None, e, 0))


class Arena:
    def __init__(self, nc, name, nbytes):
        self.t = nc.alloc_sbuf_tensor(name, [128, nbytes // 4], F32)
        self.base = 0
        self.hi = nbytes
        self.marks = []

    def mark(self):
        self.marks.append(self.base)

    def release(self):
        self.base = self.marks.pop()
        if getattr(self, "em", None) is not None:
            self.em.barrier()

    def f32(self, n):
        off = (self.base + 63) // 64 * 64
        self.base = off + n * 4
        assert self.base <= self.hi, "SBUF arena overflow"
        return self.t.ap()[:, off // 4: off // 4 + n]

    def bf16(self, n):
        off = (self.base + 63) // 64 * 64
        self.base = off + n * 2
        assert self.base <= self.hi, "SBUF arena overflow"
        return self.t.ap().bitcast(BF16)[:, off // 2: off // 2 + n]


def host_consts(L):
    c = {}
    r = np.arange(128)
    c["C_IDENT"] = np.eye(128, dtype=np.float32).astype(BF)
    c["C_EYE"] = np.eye(128, dtype=np.float32)
    c["C_LTRI1"] = (r[:, None] <= r[None, :]).astype(np.float32)
    c["C_ONESF"] = np.ones((128, 128), np.float32)
    c["C_TRI01"] = (r[:, None] <= r[None, :]).astype(np.float32)
    c["C_MASKBIG"] = np.tile((r[:, None] > r[None, :]).astype(np.float32) * NEG,
                             (1, 16)).astype(np.float32)
    c["C_ONES1"] = np.ones((1, 128), np.float32).astype(BF)
    cols = np.zeros((1, 8), np.float32)
    cols[0, :6] = [1e-5, 1.0, -0.5, -1.0, 1.0 / 2048.0, 1.0 / 1024.0]
    c["C_COLS"] = cols
    c["C_TRIU_KGT"] = (r[:, None] > r[None, :]).astype(np.float32).astype(BF)
    c["C_TRIL_KLT"] = (r[:, None] < r[None, :]).astype(np.float32).astype(BF)
    c["C_ALL1"] = np.ones((128, 128), np.float32).astype(BF)
    c["C_NEGEYE"] = (np.eye(128) * NEG).astype(BF)
    half = np.arange(0, AHD, 2, dtype=np.float64)
    inv_freq = 1.0 / 10000.0 ** (half / AHD)
    t = np.arange(L, dtype=np.float64)
    freqs = np.concatenate([t[:, None] * inv_freq[None, :]] * 2, -1)
    base_scale = (half + 0.4 * AHD) / (1.4 * AHD)
    power = (t - L // 2) / (WIN // 2)
    scale = np.concatenate([base_scale[None, :] ** power[:, None]] * 2, -1)
    cos, sin = np.cos(freqs), np.sin(freqs)
    sgn = np.concatenate([-np.ones(32), np.ones(32)])[None, :]
    c["C_CSQ"] = (cos * scale).astype(np.float32)
    c["C_SSQ"] = (sin * scale * sgn).astype(np.float32)
    c["C_CSK"] = (cos / scale).astype(np.float32)
    c["C_SSK"] = (sin / scale * sgn).astype(np.float32)
    return c


def build_program(cfg):
    L = cfg["L"]
    Bc = cfg["B"]
    depth = cfg["depth"]
    NT = L // 128
    NG = L // 512

    nc = bass.Bass("TRN2", target_bir_lowering=False)
    em = Em()
    em.sem_alias = {}
    for cn in ("C_IDENT", "C_EYE", "C_LTRI1", "C_ONESF", "C_TRI01",
               "C_TRIU_KGT", "C_TRIL_KLT", "C_ALL1", "C_NEGEYE", "C_ONES1",
               "ccol", "maskh", "convb_c", "convw_c", "dtb_r", "Aneg_r",
               "D_r"):
        em.sem_alias["d_" + cn] = "d_const"
    for old, new in (("m1x0", "ld0"), ("m1x1", "ld1"), ("m1x2", "ld2"),
                     ("m1x3", "ld3"),
                     ("a_h0", "ld0"), ("a_h1", "ld1"), ("a_h2", "ld2"),
                     ("a_h3", "ld3"), ("a_h4", "ld4"), ("a_h5", "ld5"),
                     ("a_h6", "ld6"), ("a_h7", "ld7"),
                     ("a_hs0", "ldh0"), ("a_hs1", "ldh1"), ("a_hs2", "ldh2"),
                     ("a_hs3", "ldh3"),
                     ("f_h0", "ld0"), ("f_h1", "ld1"),
                     ("f_hs0", "ldh0"), ("f_hs1", "ldh1"),
                     ("f2_h0", "ld0"), ("f2_h1", "ld1"),
                     ("f2_x10", "ld4"), ("f2_x11", "ld5"),
                     ("p2y0", "ld0"), ("p2y1", "ld1"), ("p2z0", "ld2"),
                     ("p2z1", "ld3"), ("p2x00", "ld4"), ("p2x01", "ld5"),
                     ("m1bx0", "ld0"), ("m1bx1", "ld1"),
                     ("m1bdt0", "ldh0"), ("m1bdt1", "ldh1")):
        em.sem_alias["d_" + old] = "d_" + new
    for old, new in (("m1z0", "st0"), ("m1z1", "st1"),
                     ("m1dtst0", "st2"), ("m1dtst1", "st3"),
                     ("m1by0", "st0"), ("m1by1", "st1"),
                     ("m1bct0", "st2"), ("m1bct1", "st3"),
                     ("p2x10", "st0"), ("p2x11", "st1"),
                     ("p2h00", "st2"), ("p2h01", "st3"),
                     ("a_hn0", "st0"), ("a_hn1", "st1"),
                     ("f_g0", "st0"), ("f_g1", "st1"), ("f_brs", "st2"),
                     ("f2_hn0", "st0"), ("f2_hn1", "st1")):
        em.sem_alias["d_st_" + old] = "d_st_" + new
    bufs = {}

    def B(name):
        b = bufs.get(name)
        if b is None:
            b = bufs[name] = Buf(name)
        return b

    def dma(q, out, in_, outs=(), ins=(), sem=None, **kw):
        em.op(q, lambda e: e.dma_start(out=out, in_=in_, **kw),
              ins=ins, outs=outs, dma=True, sem=sem)

    def store(q, out, in_, srcbuf, hbuf, **kw):
        em.op(q, lambda e: e.dma_start(out=out, in_=in_, **kw),
              ins=[srcbuf], outs=[hbuf], dma=True,
              sem="d_st_" + srcbuf.name, waw=False)

    consts = host_consts(L)
    ins = {}

    def inp(name, shape, dtype=F32):
        ap = nc.dram_tensor(name, list(shape), dtype, kind="ExternalInput").ap()
        ins[name] = ap
        return ap

    u = inp("L0_attn_u", (Bc, L, D))
    W_in = inp("L0_attn_W_in", (DPROJ, D))
    conv_w = inp("L0_attn_conv_w", (CONVD, 1, 4))
    conv_b = inp("L0_attn_conv_b", (CONVD,))
    dt_bias = inp("L0_attn_dt_bias", (NH,))
    A_log = inp("L0_attn_A_log", (NH,))
    D_param = inp("L0_attn_D_param", (NH,))
    norm_w = inp("L0_attn_norm_w", (DI,))
    W_out = inp("L0_attn_W_out", (D, DI))
    Wqkv = inp("L0_ffn_Wqkv", (6, D, 3 * D))
    q_scale = inp("L0_ffn_q_scale", (6, AHD))
    k_scale = inp("L0_ffn_k_scale", (6, AHD))
    Wo = inp("L0_ffn_Wo", (6, D, D))
    ln_g = inp("L0_ffn_ln_g", (6, D))
    ln_b = inp("L0_ffn_ln_b", (6, D))
    W1 = inp("L0_ffn_W1", (6, D, 2 * DFF))
    b1 = inp("L0_ffn_b1", (6, 2 * DFF))
    W2 = inp("L0_ffn_W2", (6, DFF, D))
    b2 = inp("L0_ffn_b2", (6, D))
    for name, arr in consts.items():
        inp(name, arr.shape, BF16 if arr.dtype == BF else F32)

    OUT = nc.dram_tensor("OUT", [Bc, L, D], F32, kind="ExternalOutput").ap()

    def scratch(name, shape, dtype=F32):
        return nc.dram_tensor(name, list(shape), dtype, kind="Internal").ap()

    x1_h = scratch("x1_h", (Bc, L, D))
    hA_h = scratch("hA_h", (Bc, L, D))
    hB_h = scratch("hB_h", (Bc, L, D))
    y_h = scratch("y_h", (Bc, L, DI), BF16)
    z_h = scratch("z_h", (Bc, L, DI), BF16)
    g_h = scratch("gact_h", (Bc, DFF, L), BF16)
    cumT_h = scratch("cumT_h", (2, NH, 128))
    xbc_h = scratch("xbc_h", (Bc, 18, 128, L), BF16)
    dtraw_h = scratch("dtraw_h", (Bc, L // 128, 128, NH))
    brow_h = scratch("brow_h", (5632,))
    MASKBIG_h = scratch("maskbig_h", (128, 16 * 128))

    qkvtap_h = scratch("qkvtap_h", (3, L // 128, 128, 1024), BF16)
    avtap_h = scratch("avtap_h", (L // 128, 128, 1024), BF16)
    attap_h = scratch("attap_h", (3, 128, 512), BF16)
    dbg = {}
    for name in cfg.get("debug", ()):
        src = {"x1": x1_h, "h0": hA_h, "y": y_h, "z": z_h, "hB": hB_h,
               "gact": g_h, "qkvtap": qkvtap_h, "avtap": avtap_h,
               "attap": attap_h}[name]
        dbg[name] = (nc.dram_tensor("DBG_" + name, list(src.tensor.shape),
                                    src.dtype, kind="ExternalOutput").ap(), src)

    sb = {}

    def sbt(name, shape, dtype=F32):
        t = nc.alloc_sbuf_tensor(name, list(shape), dtype)
        sb[name] = t
        return t

    ident = sbt("ident", (128, 128), BF16)
    eye = sbt("eye", (128, 128), F32)
    ltri1 = sbt("ltri1", (128, 128), F32)
    onesf = sbt("onesf", (128, 128), F32)
    tri01 = sbt("tri01", (128, 128), F32)
    ones1 = sbt("ones1", (1, 128), BF16)
    triu_kgt = sbt("triu_kgt", (128, 128), BF16)
    tril_klt = sbt("tril_klt", (128, 128), BF16)
    all1 = sbt("all1", (128, 128), BF16)
    negeye = sbt("negeye", (128, 128), BF16)
    convb_c = sbt("convb_c", (128, 18), F32)
    convw_c = sbt("convw_c", (128, 4, 18), F32)
    dtb_r = sbt("dtb_r", (128, NH), F32)
    Aneg_r = sbt("Aneg_r", (128, NH), F32)
    D_r = sbt("D_r", (128, NH), F32)
    ccol = sbt("ccol", (128, 8), F32)

    arena = Arena(nc, "arena", 202 * 1024)
    arena.em = em
    psum = [nc.alloc_psum_tensor(f"ps{i}", [128, 512], F32) for i in range(8)]
    psb = [t.ap().bitcast(BF16) for t in psum]
    PS = [B(f"ps{i}") for i in range(8)]

    for name, tile in (("C_IDENT", ident), ("C_EYE", eye), ("C_LTRI1", ltri1),
                       ("C_ONESF", onesf), ("C_TRI01", tri01),
                       ("C_TRIU_KGT", triu_kgt), ("C_TRIL_KLT", tril_klt),
                       ("C_ALL1", all1), ("C_NEGEYE", negeye)):
        dma("sp", tile[:], ins[name][:, :], outs=[B(name)])
    dma("sp", ones1[:], ins["C_ONES1"][:, :], outs=[B("C_ONES1")])
    dma("sp", ccol[:], ins["C_COLS"][0:1, :].partition_broadcast(128).squeeze(1),
        outs=[B("ccol")], allow_slow_non_contiguous=True)
    cEPS = ccol.ap()[:, 0:1]
    cONE = ccol.ap()[:, 1:2]
    cNHALF = ccol.ap()[:, 2:3]
    cNONE = ccol.ap()[:, 3:4]
    cIDI = ccol.ap()[:, 4:5]
    cID = ccol.ap()[:, 5:6]
    dma("sp", MASKBIG_h[:, :], ins["C_MASKBIG"][:, :], outs=[B("maskh")])
    dma("sp", convb_c[:], conv_b.rearrange("(c p) -> p c", p=128),
        outs=[B("convb_c")], allow_slow_non_contiguous=True)
    cw_flat = conv_w.rearrange("e o k -> (e o k)")
    for k in range(4):
        dma("sp", convw_c[:, k, :],
            cw_flat[k::4].rearrange("(c p) -> p c", p=128),
            outs=[B("convw_c")], allow_slow_non_contiguous=True)
    for row, tile, bn in ((dt_bias, dtb_r, "dtb_r"), (A_log, Aneg_r, "Aneg_r"),
                          (D_param, D_r, "D_r")):
        dma("sp", tile[:], row.unsqueeze(0).partition_broadcast(128).squeeze(1),
            outs=[B(bn)], allow_slow_non_contiguous=True)
    em.barrier()
    em.op("act", lambda e: e.activation(Aneg_r[:], Aneg_r[:], AF.Exp),
          ins=[B("Aneg_r")], outs=[B("Aneg_r")])
    em.op("dve", lambda e: e.tensor_single_scalar(Aneg_r[:], Aneg_r[:], -1.0,
                                                  ALU.mult),
          ins=[B("Aneg_r")], outs=[B("Aneg_r")])
    em.barrier()

    lnscrt = arena.f32(4 + 1024)
    lns1, lns2, lnmu, lnrs = (lnscrt[:, i:i + 1] for i in range(4))
    lnsq = lnscrt[:, 4:4 + 1024]
    LNS = [B("lns1"), B("lns2"), B("lnmu"), B("lnrs"), B("lnsq")]

    def ln_tile(xap, outap, xbuf, obuf, n=1024, eng_apply="dve"):
        em.op("dve", lambda e: e.reduce_sum(lns1, xap, AX.X),
              ins=[xbuf], outs=[LNS[0]])
        em.op("act", lambda e: e.activation(lnsq[:, 0:n], xap, AF.Square,
                                            accum_out=lns2),
              ins=[xbuf], outs=[LNS[4], LNS[1]])
        em.op("dve", lambda e: e.tensor_single_scalar(lnmu, lns1, 1.0 / n,
                                                      ALU.mult),
              ins=[LNS[0]], outs=[LNS[2]])
        em.op("dve", lambda e: e.tensor_tensor(lnrs, lnmu, lnmu, ALU.mult),
              ins=[LNS[2]], outs=[LNS[3]])
        em.op("dve", lambda e: e.scalar_tensor_tensor(lnrs, lns2, 1.0 / n, lnrs,
                                                      ALU.mult, ALU.subtract),
              ins=[LNS[1], LNS[3]], outs=[LNS[3]])
        em.op("act", lambda e: e.activation(lnrs, lnrs, AF.Ln, bias=cEPS),
              ins=[LNS[3]], outs=[LNS[3]])
        em.op("act", lambda e: e.activation(lnrs, lnrs, AF.Exp, scale=cNHALF),
              ins=[LNS[3]], outs=[LNS[3]])
        em.op(eng_apply, lambda e: e.tensor_scalar(outap, xap, lnmu, lnrs,
                                                   ALU.subtract, ALU.mult),
              ins=[LNS[2], LNS[3], xbuf], outs=[obuf])

    def transpose_pack(src_ap, src_buf, nchunk, dst3, dst_buf, tcol=None):
        def mkt(e):
            last = None
            for k in range(nchunk):
                bank = 6 + (k // 4) % 2
                col = (k % 4) * 128
                if k % 8 == 0 and k > 0:
                    pass
                last = e.transpose(psb[bank][:, col:col + 128],
                                   src_ap[:, k * 128:(k + 1) * 128], ident[:])
            return last
        for g0 in range(0, nchunk, 8):
            g1 = min(g0 + 8, nchunk)
            def mkt2(g0=g0, g1=g1):
                def f(e):
                    last = None
                    for k in range(g0, g1):
                        j = k - g0
                        bank = 6 + j // 4
                        col = (j % 4) * 128
                        last = e.transpose(psb[bank][:, col:col + 128],
                                           src_ap[:, k * 128:(k + 1) * 128],
                                           ident[:])
                    return last
                return f
            em.op("pe", mkt2(), ins=[src_buf, B("C_IDENT")], outs=[PS[6], PS[7]])
            def mkev(g0=g0, g1=g1):
                def f(e):
                    last = None
                    n1 = min(4, g1 - g0)
                    last = e.tensor_copy(dst3[:, g0:g0 + n1, :],
                                         psb[6][:, 0:n1 * 128].rearrange(
                                             "p (k c) -> p k c", k=n1))
                    if g1 - g0 > 4:
                        n2 = g1 - g0 - 4
                        last = e.tensor_copy(dst3[:, g0 + 4:g0 + 4 + n2, :],
                                             psb[7][:, 0:n2 * 128].rearrange(
                                                 "p (k c) -> p k c", k=n2))
                    return last
                return f
            em.op("dve", mkev(), ins=[PS[6], PS[7]], outs=[dst_buf])

    def prep_rowmajor_T(W, rows, wT3, wTbuf, tag):
        arena.mark()
        wtmp = arena.bf16(2 * 1024)
        wt = [wtmp[:, 0:1024], wtmp[:, 1024:2048]]
        WT = [B(tag + "0"), B(tag + "1")]
        nrow = (rows + 127) // 128
        for rc in range(nrow):
            r0, r1 = rc * 128, min(rc * 128 + 128, rows)
            nr = r1 - r0
            bb, tile = WT[rc % 2], wt[rc % 2]
            dma("pool", tile[0:nr, :], W[r0:r1, :], outs=[bb])
            def mkt(tile=tile, nr=nr):
                def f(e):
                    last = None
                    for k in range(8):
                        bank = 6 + k // 4
                        col = (k % 4) * 128
                        last = e.transpose(psb[bank][:, col:col + nr],
                                           tile[0:nr, k * 128:(k + 1) * 128],
                                           ident[0:nr, 0:nr])
                    return last
                return f
            em.op("pe", mkt(), ins=[bb, B("C_IDENT")], outs=[PS[6], PS[7]])
            def mkev(r0=r0, nr=nr):
                def f(e):
                    for half in range(2):
                        e.tensor_copy(
                            wT3[:, 4 * half:4 * half + 4, r0:r0 + nr],
                            psb[6 + half][:, 0:512].rearrange(
                                "p (k c) -> p k c", k=4)[:, :, 0:nr])
                    return e.engine_nop()
                return f
            em.op("dve", mkev(), ins=[PS[6], PS[7]], outs=[wTbuf])
        arena.release()

    def phase_m1a():
        arena.mark()
        winT = arena.bf16(8 * DPROJ)
        winT_r = winT.rearrange("p (k e) -> p k e", k=8)
        prep_rowmajor_T(W_in, DPROJ, winT_r, B("winT"), "wi")
        convdiag = arena.bf16(72 * 128)
        convdiag_r = convdiag.rearrange("p (c n) -> p c n", c=72)
        for ec in range(18):
            for k in range(4):
                def mkd(ec=ec, k=k):
                    def f(e):
                        return e.tensor_scalar(convdiag_r[:, ec * 4 + k, :],
                                               eye[:], convw_c[:, k, ec:ec + 1],
                                               None, ALU.mult)
                    return f
                em.op("dve", mkd(), ins=[B("C_EYE"), B("convw_c")],
                      outs=[B("convdiag")])
        ring = arena.bf16(18 * 516)
        ring_r = ring.rearrange("p (c n) -> p c n", c=18)
        RING = [B(f"ring{i}") for i in range(18)]
        xt = [arena.f32(1024) for _ in range(4)]
        XT = [B(f"m1x{i}") for i in range(4)]
        xn = [arena.bf16(1024) for _ in range(2)]
        XN = [B("m1xn0"), B("m1xn1")]
        xnT = [arena.bf16(8 * 512) for _ in range(2)]
        xnT_r = [t.rearrange("p (k t) -> p k t", k=8) for t in xnT]
        XNT = [B("m1xnT0"), B("m1xnT1")]
        zst = [arena.bf16(DI) for _ in range(2)]
        ZST = [B("m1z0"), B("m1z1")]
        dtst = [arena.f32(NH) for _ in range(2)]
        DTST = [B("m1dtst0"), B("m1dtst1")]

        for b in range(Bc):
            em.op("pool", lambda e: e.memset(ring_r[:, :, 0:3], 0.0), outs=RING)
            for g in range(NG):
                xnb, XNB = xn[g % 2], XN[g % 2]
                xnTb_r, XNTB = xnT_r[g % 2], XNT[g % 2]
                for i in range(4):
                    t0 = g * 512 + i * 128
                    dma("sp", xt[i][:, :], u[b, t0:t0 + 128, :], outs=[XT[i]])
                for i in range(4):
                    ln_tile(xt[i][:, :], xnb[:, :], XT[i], XNB)
                    transpose_pack(xnb, XNB, 8,
                                   xnTb_r[:, :, i * 128:(i + 1) * 128], XNTB)

                for i in range(4):
                    ti = g * 4 + i
                    def mkz(i=i, xnTb_r=xnTb_r):
                        def f(e):
                            last = None
                            for k in range(8):
                                st = xnTb_r[:, k, i * 128:(i + 1) * 128]
                                for s in range(4):
                                    e.matmul(psum[s][:, :], st,
                                             winT_r[:, k, s * 512:(s + 1) * 512],
                                             start=(k == 0), stop=(k == 7))
                                last = e.matmul(psum[4][:, 0:32], st,
                                                winT_r[:, k, DPROJ - 32:DPROJ],
                                                start=(k == 0), stop=(k == 7))
                            return last
                        return f
                    em.op("pe", mkz(), ins=[XNTB, B("winT")],
                          outs=[PS[0], PS[1], PS[2], PS[3], PS[4]])
                    zb, ZB = zst[i % 2], ZST[i % 2]
                    def mksilu(zb=zb):
                        def f(e):
                            last = None
                            for s in range(4):
                                last = e.activation(zb[:, s * 512:(s + 1) * 512],
                                                    psum[s][:, :], AF.Silu)
                            return last
                        return f
                    em.op("act", mksilu(), ins=[PS[0], PS[1], PS[2], PS[3]],
                          outs=[ZB])
                    t0 = g * 512 + i * 128
                    store("sp", z_h[b, t0:t0 + 128, :], zb[:, :], ZB, B("z_h"))
                    db, DB = dtst[i % 2], DTST[i % 2]
                    def mkdtev(db=db):
                        def f(e):
                            return e.tensor_copy(db[:, :], psum[4][:, 0:32])
                        return f
                    em.op("dve", mkdtev(), ins=[PS[4]], outs=[DB])
                    store("sp", dtraw_h[b, ti, :, :], db[:, :], DB, B("dtraw_h"))

                for ec in range(18):
                    xb_ = 4 + ec % 2
                    cb_ = 6 + ec % 2
                    def mkx(ec=ec, xnTb_r=xnTb_r, xb_=xb_):
                        def f(e):
                            last = None
                            for k in range(8):
                                last = e.matmul(
                                    psum[xb_][:, :],
                                    winT_r[:, k, DI + ec * 128:DI + ec * 128 + 128],
                                    xnTb_r[:, k, :],
                                    start=(k == 0), stop=(k == 7))
                            return last
                        return f
                    em.op("pe", mkx(), ins=[XNTB, B("winT")], outs=[PS[xb_]])
                    def mkrev(ec=ec, xb_=xb_):
                        def f(e):
                            return e.tensor_copy(ring_r[:, ec, 3:515],
                                                 psum[xb_][:, :])
                        return f
                    em.op("dve", mkrev(), ins=[PS[xb_]], outs=[RING[ec]])
                    def mkcv(ec=ec, cb_=cb_):
                        def f(e):
                            last = None
                            for k in range(4):
                                last = e.matmul(psum[cb_][:, :],
                                                convdiag_r[:, ec * 4 + k, :],
                                                ring_r[:, ec, k:k + 512],
                                                start=(k == 0), stop=(k == 3))
                            return last
                        return f
                    em.op("pe", mkcv(), ins=[RING[ec], B("convdiag")],
                          outs=[PS[cb_]])
                    def mktl(ec=ec):
                        def f(e):
                            return e.tensor_copy(ring_r[:, ec, 0:3],
                                                 ring_r[:, ec, 512:515])
                        return f
                    em.op("pool", mktl(), ins=[RING[ec]], outs=[RING[ec]])
                    def mks(ec=ec, cb_=cb_):
                        def f(e):
                            return e.activation(ring_r[:, ec, 3:515],
                                                psum[cb_][:, :], AF.Silu,
                                                bias=convb_c[:, ec:ec + 1])
                        return f
                    em.op("act", mks(), ins=[PS[cb_], B("convb_c"), RING[ec]],
                          outs=[RING[ec]])
                    def mkso(ec=ec, b=b, g=g):
                        def f(e):
                            return e.dma_start(
                                out=xbc_h[b, ec, :, g * 512:(g + 1) * 512],
                                in_=ring_r[:, ec, 3:515])
                        return f
                    em.op("sp", mkso(), ins=[RING[ec]], outs=[B("xbc_h")],
                          dma=True, sem=f"d_st_ring{ec}", waw=False)
        em.barrier()
        arena.release()

    def phase_m1b():
        arena.mark()
        ddiag = arena.bf16(32 * 128)
        ddiag_r = ddiag.rearrange("p (h n) -> p h n", h=32)
        for h in range(NH):
            def mkd(h=h):
                def f(e):
                    return e.tensor_scalar(ddiag_r[:, h, :], eye[:],
                                           D_r[:, h:h + 1], None, ALU.mult)
                return f
            em.op("dve", mkd(), ins=[B("C_EYE"), B("D_r")], outs=[B("ddiag")])
        xc = [arena.bf16(18 * 128) for _ in range(2)]
        xc_r = [t.rearrange("p (c n) -> p c n", c=18) for t in xc]
        XC = [B("m1bx0"), B("m1bx1")]
        s_f32 = arena.f32(DI)
        SF = B("m1bsf")
        s_bf = arena.bf16(DI)
        SB = B("m1bsb")
        dts = [arena.f32(8 * NH) for _ in range(2)]
        dts_r = [t.rearrange("p (a h) -> p a h", a=8) for t in dts]
        DTS = [B("m1bdt0"), B("m1bdt1")]
        cumT_sb = [arena.f32(128) for _ in range(2)]
        CUMT = [B("m1bct0"), B("m1bct1")]
        cumrep = [arena.f32(2048) for _ in range(2)]
        CUMREP = [B("m1bcr0"), B("m1bcr1")]
        preL = [arena.bf16(2048) for _ in range(2)]
        PREL = [B("m1bpl0"), B("m1bpl1")]
        expL = [arena.bf16(2048) for _ in range(2)]
        EXPL = [B("m1bel0"), B("m1bel1")]
        mprime = arena.bf16(4096)
        mprime_r = mprime.rearrange("p (h t) -> p h t", h=32)
        MPR = B("m1bmp")
        xTt = [arena.bf16(2048) for _ in range(2)]
        XTT = [B("m1bxT0"), B("m1bxT1")]
        xTw = arena.bf16(2048)
        XTW = B("m1bxw")
        gm = arena.bf16(128)
        GM = B("m1bgm")
        bT = arena.bf16(128)
        BT = B("m1bbT")
        t1c = arena.f32(2048)
        T1C = B("m1bt1")
        yout = [arena.bf16(DI) for _ in range(2)]
        YOUT = [B("m1by0"), B("m1by1")]

        for b in range(Bc):
            em.op("dve", lambda e: e.memset(s_f32, 0.0), outs=[SF])
            em.op("pool", lambda e: e.memset(s_bf, 0.0), outs=[SB])
            for ci in range(L // 128):
                c0 = ci * 128
                xcb, xcb_r, XCB = xc[ci % 2], xc_r[ci % 2], XC[ci % 2]
                dtr, DTR = dts_r[ci % 2], DTS[ci % 2]
                for ec in range(18):
                    def mkl(ec=ec, b=b, c0=c0, xcb_r=xcb_r):
                        def f(e):
                            return e.dma_start(out=xcb_r[:, ec, :],
                                               in_=xbc_h[b, ec, :, c0:c0 + 128])
                        return f
                    em.op("sp", mkl(), ins=[B("xbc_h")], outs=[XCB],
                          dma=True, waw=(ec == 0))
                def mkld(b=b, ci=ci, dtr=dtr):
                    def f(e):
                        return e.dma_start(out=dtr[:, 0, :],
                                           in_=dtraw_h[b, ci, :, :])
                    return f
                em.op("sp", mkld(), ins=[B("dtraw_h")], outs=[DTR], dma=True)

                raw = dtr[:, 0, :]
                dtv = dtr[:, 1, :]
                dtA = dtr[:, 2, :]
                cum = dtr[:, 3, :]
                ecum = dtr[:, 4, :]
                wv = dtr[:, 5, :]
                ct = dtr[:, 6, :]
                ecq = dtr[:, 7, :]
                em.op("dve", lambda e, raw=raw: e.tensor_tensor(
                    raw, raw, dtb_r[:], ALU.add),
                    ins=[DTR, B("dtb_r")], outs=[DTR])
                em.op("act", lambda e, raw=raw, dtv=dtv: e.activation(
                    dtv, raw, AF.Abs), ins=[DTR], outs=[DTR])
                em.op("act", lambda e, dtv=dtv: e.activation(
                    dtv, dtv, AF.Exp, scale=cNONE), ins=[DTR], outs=[DTR])
                em.op("act", lambda e, dtv=dtv: e.activation(
                    dtv, dtv, AF.Ln, bias=cONE), ins=[DTR], outs=[DTR])
                em.op("dve", lambda e, raw=raw, dtv=dtv: e.scalar_tensor_tensor(
                    dtv, raw, 0.0, dtv, ALU.max, ALU.add),
                    ins=[DTR], outs=[DTR])
                em.op("dve", lambda e, dtv=dtv, dtA=dtA: e.tensor_tensor(
                    dtA, dtv, Aneg_r[:], ALU.mult),
                    ins=[DTR, B("Aneg_r")], outs=[DTR])
                def mkcum(dtA=dtA):
                    def f(e):
                        e.matmul(psum[4][:, 64:96], ltri1[:], dtA,
                                 start=True, stop=True)
                        return e.matmul(psum[4][:, 96:128], onesf[:], dtA,
                                        start=True, stop=True)
                    return f
                em.op("pe", mkcum(), ins=[DTR, B("C_LTRI1"), B("C_ONESF")],
                      outs=[PS[4]])
                em.op("dve", lambda e, cum=cum: e.tensor_copy(
                    cum, psum[4][:, 64:96]), ins=[PS[4]], outs=[DTR])
                em.op("act", lambda e, ecq=ecq: e.activation(
                    ecq, psum[4][:, 96:128], AF.Exp), ins=[PS[4]], outs=[DTR])
                em.op("act", lambda e, ecum=ecum, cum=cum: e.activation(
                    ecum, cum, AF.Exp), ins=[DTR], outs=[DTR])
                em.op("dve", lambda e, wv=wv: e.tensor_copy(
                    wv, psum[4][:, 96:128]), ins=[PS[4]], outs=[DTR])
                em.op("dve", lambda e, wv=wv, cum=cum: e.tensor_tensor(
                    wv, wv, cum, ALU.subtract), ins=[DTR], outs=[DTR])
                em.op("act", lambda e, wv=wv: e.activation(
                    wv, wv, AF.Exp), ins=[DTR], outs=[DTR])
                em.op("dve", lambda e, wv=wv, dtv=dtv: e.tensor_tensor(
                    wv, wv, dtv, ALU.mult), ins=[DTR], outs=[DTR])
                em.op("act", lambda e, ct=ct, dtv=dtv: e.activation(
                    ct, dtv, AF.Ln), ins=[DTR], outs=[DTR])
                em.op("dve", lambda e, ct=ct, cum=cum: e.tensor_tensor(
                    ct, ct, cum, ALU.subtract), ins=[DTR], outs=[DTR])
                def mkct2(cum=cum):
                    def f(e):
                        return e.matmul(psum[5][0:32, 0:128], cum, eye[:],
                                        start=True, stop=True)
                    return f
                em.op("pe", mkct2(), ins=[DTR, B("C_EYE")], outs=[PS[5]])
                cts, CTS = cumT_sb[ci % 2], CUMT[ci % 2]
                def mkcte(cts=cts):
                    def f(e):
                        return e.tensor_copy(cts[0:32, :], psum[5][0:32, 0:128])
                    return f
                em.op("dve", mkcte(), ins=[PS[5]], outs=[CTS])
                store("sp", cumT_h[ci % 2, :, :], cts[0:32, :], CTS,
                      B(f"cumTh{ci % 2}"))

                xb, XB = xTt[ci % 2], XTT[ci % 2]
                def mkxt(xcb_r=xcb_r):
                    def f(e):
                        last = None
                        for cch in range(16):
                            bank = 6 + cch // 8
                            col = (cch % 8) * 128
                            last = e.transpose(psb[bank][:, col:col + 128],
                                               xcb_r[:, cch, :], ident[:])
                        return last
                    return f
                em.op("pe", mkxt(), ins=[XCB, B("C_IDENT")], outs=[PS[6], PS[7]])
                def mkxe(xb=xb):
                    def f(e):
                        e.tensor_copy(xb[:, 0:1024], psb[6][:, 0:1024])
                        return e.tensor_copy(xb[:, 1024:2048], psb[7][:, 0:1024])
                    return f
                em.op("dve", mkxe(), ins=[PS[6], PS[7]], outs=[XB])
                def mkxw(xb=xb, wv=wv):
                    def f(e):
                        return e.tensor_tensor(
                            xTw.rearrange("p (h d) -> p h d", h=32),
                            xb.rearrange("p (h d) -> p h d", h=32),
                            wv.unsqueeze(2).broadcast_to([128, 32, 64]),
                            ALU.mult)
                    return f
                em.op("pool", mkxw(), ins=[XB, DTR], outs=[XTW])
                def mkbt(xcb_r=xcb_r):
                    def f(e):
                        return e.transpose(psb[5][:, 0:128], xcb_r[:, 16, :],
                                           ident[:])
                    return f
                em.op("pe", mkbt(), ins=[XCB, B("C_IDENT")], outs=[PS[5]])
                em.op("dve", lambda e: e.tensor_copy(bT[:, :], psb[5][:, 0:128]),
                      ins=[PS[5]], outs=[BT])
                def mkg(xcb_r=xcb_r):
                    def f(e):
                        return e.matmul(psum[5][:, 128:256], xcb_r[:, 16, :],
                                        xcb_r[:, 17, :], start=True, stop=True)
                    return f
                em.op("pe", mkg(), ins=[XCB, BT], outs=[PS[5]])
                em.op("dve", lambda e: e.tensor_tensor(
                    gm[:, :], psum[5][:, 128:256], tri01[:], ALU.mult),
                    ins=[PS[5], B("C_TRI01")], outs=[GM])
                for half in range(2):
                    cr, CR = cumrep[half], CUMREP[half]
                    pl, PL = preL[half], PREL[half]
                    el, EL = expL[half], EXPL[half]
                    for h in range(16):
                        hh = half * 16 + h
                        def mkbc(hh=hh, h=h, ci=ci, cr=cr):
                            def f(e):
                                return e.dma_start(
                                    out=cr[:, h * 128:(h + 1) * 128],
                                    in_=cumT_h[ci % 2, hh, :].unsqueeze(0)
                                    .partition_broadcast(128).squeeze(1))
                            return f
                        em.op("sp", mkbc(), ins=[B(f"cumTh{ci % 2}")],
                              outs=[CR], dma=True, waw=False,
                              sem="d_cr%d" % half)
                    em.op("pool", lambda e, cr=cr: e.dma_start(
                        out=cr[:, :], in_=MASKBIG_h[:, 0:2048],
                        accum_op=ALU.add),
                        ins=[B("maskh")], outs=[CR], dma=True,
                        sem="d_g_cr%d" % half)
                    def mkpre(half=half, ct=ct, pl=pl, cr=cr):
                        def f(e):
                            return e.tensor_tensor(
                                pl.rearrange("p (h t) -> p h t", h=16),
                                cr.rearrange("p (h t) -> p h t", h=16),
                                ct[:, half * 16:half * 16 + 16].unsqueeze(2)
                                .broadcast_to([128, 16, 128]),
                                ALU.add)
                        return f
                    em.op("dve", mkpre(), ins=[CR, DTR], outs=[PL])
                    em.op("act", lambda e, el=el, pl=pl: e.activation(
                        el[:, :], pl[:, :], AF.Exp), ins=[PL], outs=[EL])
                    def mkmp(half=half, el=el):
                        def f(e):
                            return e.tensor_tensor(
                                mprime_r[:, half * 16:half * 16 + 16, :],
                                el.rearrange("p (h t) -> p h t", h=16),
                                gm.unsqueeze(1).broadcast_to([128, 16, 128]),
                                ALU.mult)
                        return f
                    em.op("pool", mkmp(), ins=[EL, GM], outs=[MPR])
                def mky(xb=xb):
                    def f(e):
                        last = None
                        for h in range(NH):
                            bank = h // 8
                            col = (h % 8) * 64
                            e.matmul(psum[bank][:, col:col + 64],
                                     mprime_r[:, h, :],
                                     xb[:, h * 64:(h + 1) * 64],
                                     start=True, stop=False)
                            last = e.matmul(psum[bank][:, col:col + 64],
                                            ddiag_r[:, h, :],
                                            xb[:, h * 64:(h + 1) * 64],
                                            start=False, stop=True)
                        return last
                    return f
                em.op("pe", mky(), ins=[MPR, XB, B("ddiag")],
                      outs=[PS[0], PS[1], PS[2], PS[3]])
                def mkzm(xcb_r=xcb_r):
                    def f(e):
                        last = None
                        for h in range(NH):
                            bank = 4 + h // 8
                            col = (h % 8) * 64
                            last = e.matmul(psum[bank][:, col:col + 64],
                                            xcb_r[:, 17, :],
                                            s_bf[:, h * 64:(h + 1) * 64],
                                            start=True, stop=True)
                        return last
                    return f
                em.op("pe", mkzm(), ins=[XCB, SB],
                      outs=[PS[4], PS[5], PS[6], PS[7]])
                yo, YO = yout[ci % 2], YOUT[ci % 2]
                def mkt1(ecum=ecum):
                    def f(e):
                        last = None
                        for q in range(4):
                            last = e.tensor_tensor(
                                t1c.rearrange("p (h d) -> p h d", h=32)[
                                    :, q * 8:(q + 1) * 8, :],
                                psum[4 + q][:, :].rearrange(
                                    "p (h d) -> p h d", h=8),
                                ecum[:, q * 8:(q + 1) * 8].unsqueeze(2)
                                .broadcast_to([128, 8, 64]),
                                ALU.mult)
                        return last
                    return f
                em.op("dve", mkt1(), ins=[PS[4], PS[5], PS[6], PS[7], DTR],
                      outs=[T1C])
                def mkyf(yo=yo):
                    def f(e):
                        last = None
                        for q in range(4):
                            last = e.tensor_tensor(yo[:, q * 512:(q + 1) * 512],
                                                   t1c[:, q * 512:(q + 1) * 512],
                                                   psum[q][:, :], ALU.add)
                        return last
                    return f
                em.op("dve", mkyf(), ins=[T1C, PS[0], PS[1], PS[2], PS[3]],
                      outs=[YO])
                store("sp", y_h[b, c0:c0 + 128, :], yo[:, :], YO, B("y_h"))
                def mksm():
                    def f(e):
                        last = None
                        for h in range(NH):
                            bank = 4 + h // 8
                            col = (h % 8) * 64
                            last = e.matmul(psum[bank][:, col:col + 64],
                                            bT[:, :],
                                            xTw[:, h * 64:(h + 1) * 64],
                                            start=True, stop=True)
                        return last
                    return f
                em.op("pe", mksm(), ins=[BT, XTW],
                      outs=[PS[4], PS[5], PS[6], PS[7]])
                def mkdec(ecq=ecq):
                    def f(e):
                        return e.tensor_tensor(
                            s_f32.rearrange("p (h d) -> p h d", h=32),
                            s_f32.rearrange("p (h d) -> p h d", h=32),
                            ecq.unsqueeze(2).broadcast_to([128, 32, 64]),
                            ALU.mult)
                    return f
                em.op("pool", mkdec(), ins=[SF, DTR], outs=[SF])
                def mkadd():
                    def f(e):
                        last = None
                        for q in range(4):
                            last = e.tensor_tensor(
                                s_f32[:, q * 512:(q + 1) * 512],
                                s_f32[:, q * 512:(q + 1) * 512],
                                psum[4 + q][:, :], ALU.add)
                        return last
                    return f
                em.op("dve", mkadd(), ins=[SF, PS[4], PS[5], PS[6], PS[7]],
                      outs=[SF])
                em.op("pool", lambda e: e.tensor_copy(s_bf[:, :], s_f32[:, :]),
                      ins=[SF], outs=[SB])
        em.barrier()
        arena.release()

    def xnT_rview(xnTb_r, i):
        return xnTb_r[:, :, i * 128:(i + 1) * 128]

    def phase_mamba2():
        stage = cfg.get("m2_stage", 9)
        arena.mark()
        normw_r = arena.f32(DI)
        dma("sp", normw_r[:, :],
            norm_w.unsqueeze(0).partition_broadcast(128).squeeze(1),
            outs=[B("normw_r")], allow_slow_non_contiguous=True)
        woutT = arena.bf16(16 * D)
        woutT_r = woutT.rearrange("p (k o) -> p k o", k=16)
        prep_rowmajor_T2(W_out, woutT_r, B("woutT"))
        yt = [arena.bf16(DI) for _ in range(2)]
        YT = [B("p2y0"), B("p2y1")]
        zt = [arena.bf16(DI) for _ in range(2)]
        ZT = [B("p2z0"), B("p2z1")]
        x0t = [arena.f32(D) for _ in range(2)]
        X0T = [B("p2x00"), B("p2x01")]
        yg = [arena.f32(DI) for _ in range(2)]
        YG = [B("p2yg0"), B("p2yg1")]
        yn = [arena.bf16(DI) for _ in range(2)]
        YN = [B("p2yn0"), B("p2yn1")]
        ynT = [arena.bf16(16 * 128) for _ in range(2)]
        ynT_r = [t.rearrange("p (k t) -> p k t", k=16) for t in ynT]
        YNT = [B("p2ynT0"), B("p2ynT1")]
        x1t = [arena.f32(D) for _ in range(2)]
        X1T = [B("p2x10"), B("p2x11")]
        h0t = [arena.f32(D) for _ in range(2)]
        H0T = [B("p2h00"), B("p2h01")]
        sc = arena.f32(6 + DI)
        s1, s2, s3, s4, smu, srs = (sc[:, i:i + 1] for i in range(6))
        ssq = sc[:, 6:6 + DI]
        SS = [B(f"p2s{i}") for i in range(6)]

        for b in range(Bc):
            for ti in range(NT):
                t0 = ti * 128
                ybuf, YB = yt[ti % 2], YT[ti % 2]
                zbuf, ZB = zt[ti % 2], ZT[ti % 2]
                xbuf, XB = x0t[ti % 2], X0T[ti % 2]
                x1buf, X1B = x1t[ti % 2], X1T[ti % 2]
                h0buf, H0B = h0t[ti % 2], H0T[ti % 2]
                dma("sp", ybuf[:, :], y_h[b, t0:t0 + 128, :], outs=[YB],
                    ins=[B("y_h")])
                dma("sp", zbuf[:, :], z_h[b, t0:t0 + 128, :], outs=[ZB],
                    ins=[B("z_h")])
                dma("sp", xbuf[:, :], u[b, t0:t0 + 128, :], outs=[XB])
                ygb, YGB = yg[ti % 2], YG[ti % 2]
                ynb, YNB = yn[ti % 2], YN[ti % 2]
                ynTb_r, YNTB = ynT_r[ti % 2], YNT[ti % 2]
                mb0 = 2 * (ti % 2)
                em.op("pool", lambda e, ybuf=ybuf, zbuf=zbuf, ygb=ygb: e.tensor_tensor(
                    ygb[:, :], ybuf[:, :], zbuf[:, :], ALU.mult),
                    ins=[YB, ZB], outs=[YGB])
                if stage < 1:
                    continue
                em.op("act", lambda e, ygb=ygb: e.activation(ssq, ygb[:, :], AF.Square,
                                                             accum_out=s2),
                      ins=[YGB], outs=[SS[1]])
                em.op("act", lambda e: e.activation(srs, s2, AF.Ln, bias=cEPS,
                                                    scale=cIDI),
                      ins=[SS[1]], outs=[SS[5]])
                em.op("act", lambda e: e.activation(srs, srs, AF.Exp, scale=cNHALF),
                      ins=[SS[5]], outs=[SS[5]])
                em.op("dve", lambda e, ynb=ynb, ygb=ygb: e.scalar_tensor_tensor(
                    ynb[:, :], ygb[:, :], srs, normw_r[:], ALU.mult, ALU.mult),
                    ins=[YGB, SS[5], B("normw_r")], outs=[YNB])
                if stage < 2:
                    continue
                transpose_pack(ynb, YNB, 16, ynTb_r, YNTB)
                if stage < 3:
                    continue
                def mkmm(e, ynTb_r=ynTb_r, mb0=mb0):
                    last = None
                    for k in range(16):
                        for s_ in range(2):
                            last = e.matmul(psum[mb0 + s_][:, :],
                                            ynTb_r[:, k, :],
                                            woutT_r[:, k, s_ * 512:(s_ + 1) * 512],
                                            start=(k == 0), stop=(k == 15))
                    return last
                em.op("pe", mkmm, ins=[YNTB, B("woutT")],
                      outs=[PS[mb0], PS[mb0 + 1]])
                if stage < 4:
                    continue
                def mkst(e, mb0=mb0):
                    e.activation(ssq[:, 0:512], psum[mb0][:, :], AF.Copy,
                                 accum_out=s1)
                    return e.activation(ssq[:, 512:1024], psum[mb0 + 1][:, :],
                                        AF.Copy, accum_out=s3)
                em.op("act", mkst, ins=[PS[mb0], PS[mb0 + 1]],
                      outs=[SS[0], SS[2], SS[1]])
                if stage < 5:
                    continue
                em.op("dve", lambda e: e.tensor_tensor(s1, s1, s3, ALU.add),
                      ins=[SS[0], SS[2]], outs=[SS[0]])
                def mksq(e, mb0=mb0):
                    e.activation(ssq[:, 0:512], psum[mb0][:, :], AF.Square,
                                 accum_out=s2)
                    return e.activation(ssq[:, 512:1024], psum[mb0 + 1][:, :],
                                        AF.Square, accum_out=s4)
                em.op("act", mksq, ins=[PS[mb0], PS[mb0 + 1]],
                      outs=[SS[1], SS[3]])
                if stage < 6:
                    continue
                em.op("dve", lambda e: e.tensor_tensor(s2, s2, s4, ALU.add),
                      ins=[SS[1], SS[3]], outs=[SS[1]])
                em.op("dve", lambda e: e.tensor_single_scalar(smu, s1, 1.0 / D,
                                                              ALU.mult),
                      ins=[SS[0]], outs=[SS[4]])
                em.op("dve", lambda e: e.tensor_tensor(srs, smu, smu, ALU.mult),
                      ins=[SS[4]], outs=[SS[5]])
                em.op("dve", lambda e: e.scalar_tensor_tensor(
                    srs, s2, 1.0 / D, srs, ALU.mult, ALU.subtract),
                    ins=[SS[1], SS[5]], outs=[SS[5]])
                em.op("act", lambda e: e.activation(srs, srs, AF.Ln, bias=cEPS),
                      ins=[SS[5]], outs=[SS[5]])
                em.op("act", lambda e: e.activation(srs, srs, AF.Exp, scale=cNHALF),
                      ins=[SS[5]], outs=[SS[5]])
                def mkx1(e, x1buf=x1buf, mb0=mb0):
                    last = None
                    for s_ in range(2):
                        last = e.tensor_scalar(x1buf[:, s_ * 512:(s_ + 1) * 512],
                                               psum[mb0 + s_][:, :], smu, srs,
                                               ALU.subtract, ALU.mult)
                    return last
                em.op("dve", mkx1, ins=[PS[mb0], PS[mb0 + 1], SS[4], SS[5]],
                      outs=[X1B])
                em.op("dve", lambda e, x1buf=x1buf, xbuf=xbuf: e.tensor_tensor(
                    x1buf[:, :], x1buf[:, :], xbuf[:, :], ALU.add),
                    ins=[X1B, XB], outs=[X1B])
                store("sp", x1_h[b, t0:t0 + 128, :], x1buf[:, :], X1B, B("x1_h"))
                ln_tile(x1buf[:, :], h0buf[:, :], X1B, H0B)
                store("sp", hA_h[b, t0:t0 + 128, :], h0buf[:, :], H0B, B("hA_h"))
        em.barrier()
        arena.release()

    def prep_rowmajor_T2(W, wT3, wTbuf):
        arena.mark()
        wtmp = arena.bf16(2 * DI)
        wt = [wtmp[:, 0:DI], wtmp[:, DI:2 * DI]]
        WT = [B("wo0"), B("wo1")]
        for rc in range(8):
            bb, tile = WT[rc % 2], wt[rc % 2]
            dma("pool", tile[:, :], W[rc * 128:(rc + 1) * 128, :], outs=[bb])
            def mkt(tile=tile):
                def f(e):
                    last = None
                    for k in range(16):
                        bank = 6 + k // 8
                        col = (k % 8) * 128
                        last = e.transpose(psb[bank][:, col:col + 128],
                                           tile[:, k * 128:(k + 1) * 128],
                                           ident[:])
                    return last
                return f
            em.op("pe", mkt(), ins=[bb, B("C_IDENT")], outs=[PS[6], PS[7]])
            def mkev(rc=rc):
                def f(e):
                    for half in range(2):
                        e.tensor_copy(
                            wT3[:, 8 * half:8 * half + 8,
                                rc * 128:(rc + 1) * 128],
                            psb[6 + half][:, 0:1024].rearrange(
                                "p (k c) -> p k c", k=8))
                    return e.engine_nop()
                return f
            em.op("dve", mkev(), ins=[PS[6], PS[7]], outs=[wTbuf])
        arena.release()

    def phase_attn(l, hsrc, hdst, HSRC, HDST):
        arena.mark()
        NTL = L // 128
        wqkv = arena.bf16(8 * 3072)
        wqkv_r = wqkv.rearrange("p (k e) -> p k e", k=8)
        WQ = B("wqkv")
        for k in range(8):
            em.op("pool", lambda e, k=k: e.dma_start(
                out=wqkv_r[:, k, :], in_=Wqkv[l, k * 128:(k + 1) * 128, :]),
                ins=[], outs=[WQ], dma=True, sem="d_wqkv", waw=(k == 0))
        wo = arena.bf16(8 * 1024)
        wo_r = wo.rearrange("p (k e) -> p k e", k=8)
        WOB = B("wob")
        for k in range(8):
            em.op("pool", lambda e, k=k: e.dma_start(
                out=wo_r[:, k, :], in_=Wo[l, k * 128:(k + 1) * 128, :]),
                ins=[], outs=[WOB], dma=True, sem="d_wob", waw=(k == 0))
        qsrep = arena.f32(64)
        ksrep = arena.f32(64)
        QSC, KSC = B("qsrep"), B("ksrep")
        dma("sp", qsrep[:, :], q_scale[l:l + 1, :].partition_broadcast(128)
            .squeeze(1), outs=[QSC], allow_slow_non_contiguous=True)
        dma("sp", ksrep[:, :], k_scale[l:l + 1, :].partition_broadcast(128)
            .squeeze(1), outs=[KSC], allow_slow_non_contiguous=True)
        base = arena.f32(NTL * 32)
        BASE = B("tblbase")
        tbl = {nm: arena.bf16(NTL * 64) for nm in ("csq", "ssq", "csk", "ssk")}
        TBL = B("tbl")
        for nm, srcn, rep, repbuf in (("csq", "C_CSQ", qsrep, QSC),
                                      ("ssq", "C_SSQ", qsrep, QSC),
                                      ("csk", "C_CSK", ksrep, KSC),
                                      ("ssk", "C_SSK", ksrep, KSC)):
            NH2 = NTL // 2
            for hf in range(2):
                dma("sp", base.rearrange("p (a d) -> p a d", a=NH2),
                    ins[srcn].rearrange("(a p) d -> p a d", p=128)[
                        :, hf * NH2:(hf + 1) * NH2, :],
                    outs=[BASE], sem="d_tblbase")
                dst = tbl[nm].rearrange("p (a d) -> p a d", a=NTL)[
                    :, hf * NH2:(hf + 1) * NH2, :]
                bsr = base.rearrange("p (a d) -> p a d", a=NH2)
                if nm.startswith("cs"):
                    def mk(dst=dst, bsr=bsr, rep=rep):
                        def f(e):
                            return e.tensor_tensor(
                                dst, bsr, rep.unsqueeze(1)
                                .broadcast_to([128, NH2, 64]), ALU.mult)
                        return f
                    em.op("dve", mk(), ins=[BASE, repbuf], outs=[TBL])
                else:
                    def mk1(dst=dst, bsr=bsr, rep=rep):
                        def f(e):
                            return e.tensor_tensor(
                                dst[:, :, 0:32], bsr[:, :, 0:32],
                                rep[:, 32:64].unsqueeze(1)
                                .broadcast_to([128, NH2, 32]), ALU.mult)
                        return f
                    def mk2(dst=dst, bsr=bsr, rep=rep):
                        def f(e):
                            return e.tensor_tensor(
                                dst[:, :, 32:64], bsr[:, :, 32:64],
                                rep[:, 0:32].unsqueeze(1)
                                .broadcast_to([128, NH2, 32]), ALU.mult)
                        return f
                    em.op("dve", mk1(), ins=[BASE, repbuf], outs=[TBL])
                    em.op("dve", mk2(), ins=[BASE, repbuf], outs=[TBL])
        csq_r = tbl["csq"].rearrange("p (a d) -> p a d", a=NTL)
        ssq_r = tbl["ssq"].rearrange("p (a d) -> p a d", a=NTL)
        csk_r = tbl["csk"].rearrange("p (a d) -> p a d", a=NTL)
        ssk_r = tbl["ssk"].rearrange("p (a d) -> p a d", a=NTL)

        kT_ring = arena.bf16(8 * 768)
        kT_r = kT_ring.rearrange("p (c t) -> p c t", c=8)
        KTR = B("kT_ring")
        v_ring = arena.bf16(6 * 1024)
        v_r = v_ring.rearrange("p (s d) -> p s d", s=6)
        VRING = B("v_ring")
        qT = [arena.bf16(8 * 512) for _ in range(2)]
        qT_r = [t.rearrange("p (c t) -> p c t", c=8) for t in qT]
        QT = [B("qT0"), B("qT1")]
        ht = [arena.f32(1024) for _ in range(4)]
        HT = [B(f"a_h{i}") for i in range(4)]
        hsh = [arena.f32(512) for _ in range(4)]
        HSH = [B(f"a_hs{i}") for i in range(4)]
        xs = [arena.bf16(1024) for _ in range(2)]
        XS = [B("a_xs0"), B("a_xs1")]
        xsT = [arena.bf16(8 * 128) for _ in range(2)]
        xsT_r = [t.rearrange("p (c t) -> p c t", c=8) for t in xsT]
        XST = [B("a_xsT0"), B("a_xsT1")]
        qrot = [arena.bf16(1024) for _ in range(2)]
        QR = [B("a_qr0"), B("a_qr1")]
        k1b = [arena.bf16(1024) for _ in range(2)]
        K1B = [B("a_k10"), B("a_k11")]
        krot = [arena.bf16(1024) for _ in range(2)]
        KR = [B("a_kr0"), B("a_kr1")]
        t2b = [arena.bf16(1024) for _ in range(2)]
        T2B = [B("a_t20"), B("a_t21")]
        sqscr = [arena.f32(1024)]
        SQS = [B("a_sq0")]
        rcol = [arena.f32(16) for _ in range(2)]
        RCOL = [B("a_rcol0"), B("a_rcol1")]
        rsq8 = arena.f32(4 * 16)
        rsq8_r = rsq8.rearrange("p (a h) -> p a h", a=4)
        RSQ8 = B("a_rsq8")
        nrsq8 = arena.f32(4 * 16)
        nrsq8_r = nrsq8.rearrange("p (a h) -> p a h", a=4)
        NRSQ8 = B("a_nrsq8")
        rsk = [arena.f32(16) for _ in range(2)]
        RSK = [B("a_rsk0"), B("a_rsk1")]
        scol = [arena.f32(4) for _ in range(2)]
        SCOL = [B("a_scol0"), B("a_scol1")]
        attns = [[arena.bf16(512) for _ in range(2)] for _ in range(2)]
        ATT = [[B("a_at00"), B("a_at01")], [B("a_at10"), B("a_at11")]]
        diag = [arena.bf16(128) for _ in range(2)]
        DIAG = [B("a_diag0"), B("a_diag1")]
        attnT = [arena.bf16(512) for _ in range(2)]
        ATT_T = [B("a_attT0"), B("a_attT1")]
        av = [arena.bf16(8 * 128) for _ in range(2)]
        av_r = [t.rearrange("p (c t) -> p c t", c=8) for t in av]
        AV = [B("a_av0"), B("a_av1")]
        hnew = [arena.f32(1024) for _ in range(2)]
        HN = [B("a_hn0"), B("a_hn1")]

        em.op("pool", lambda e: e.memset(kT_ring[:, :], 0.0), outs=[KTR])
        em.op("pool", lambda e: e.memset(v_ring[:, :], 0.0), outs=[VRING])
        for su in range(2):
            for pr in range(2):
                em.op("pool", lambda e, su=su, pr=pr: e.memset(
                    attns[su][pr][:, :], 0.0), outs=[ATT[su][pr]])

        for b in range(Bc):
            if b > 0:
                for pr in range(2):
                    em.op("pool", lambda e, pr=pr: e.memset(
                        attns[0][pr][:, 0:256], 0.0), outs=[ATT[0][pr]])
                    em.op("pool", lambda e, pr=pr: e.memset(
                        attns[1][pr][:, 128:256], 0.0), outs=[ATT[1][pr]])
            for g in range(NG):
                qTb_r, QTB = qT_r[g % 2], QT[g % 2]
                for i in range(4):
                    t0 = g * 512 + i * 128
                    hb, HB = ht[i], HT[i]
                    dma("sp", hb[:, :], hsrc[b, t0:t0 + 128, :], outs=[HB],
                        ins=[HSRC])
                    if t0 == 0:
                        em.op("dve", lambda e, i=i: e.memset(hsh[i][0:1, :], 0.0),
                              outs=[HSH[i]])
                        dma("sp", hsh[i][1:128, :], hsrc[b, 0:127, 512:1024],
                            outs=[HSH[i]], ins=[HSRC])
                    else:
                        dma("sp", hsh[i][:, :],
                            hsrc[b, t0 - 1:t0 + 127, 512:1024],
                            outs=[HSH[i]], ins=[HSRC])
                for i in range(4):
                    gt = g * 4 + i
                    par = i % 2
                    xsb, XSB = xs[par], XS[par]
                    xsT_rb, XSTB = xsT_r[par], XST[par]
                    qrb, QRB = qrot[par], QR[par]
                    k1bb, K1BB = k1b[par], K1B[par]
                    krb, KRB = krot[par], KR[par]
                    t2bb, T2BB = t2b[par], T2B[par]
                    sqb, SQSB = sqscr[0], SQS[0]
                    rcb, RCOLB = rcol[par], RCOL[par]
                    rskb, RSKB = rsk[par], RSK[par]
                    hb, HB = ht[i], HT[i]
                    em.op("dve", lambda e, hb=hb, xsb=xsb: e.tensor_copy(
                        xsb[:, 0:512], hb[:, 0:512]), ins=[HB], outs=[XSB])
                    em.op("dve", lambda e, i=i, xsb=xsb: e.tensor_copy(
                        xsb[:, 512:1024], hsh[i][:, :]), ins=[HSH[i]], outs=[XSB])
                    transpose_pack(xsb, XSB, 8, xsT_rb, XSTB)
                    def mkqkv(e, xsT_rb=xsT_rb):
                        last = None
                        for k in range(8):
                            st = xsT_rb[:, k, :]
                            for s in range(6):
                                last = e.matmul(
                                    psum[s][:, :], st,
                                    wqkv_r[:, k, s * 512:(s + 1) * 512],
                                    start=(k == 0), stop=(k == 7))
                        return last
                    em.op("pe", mkqkv, ins=[XSTB, WQ],
                          outs=[PS[0], PS[1], PS[2], PS[3], PS[4], PS[5]])
                    def mkv(e, gt=gt):
                        e.tensor_copy(v_r[:, gt % 6, 0:512], psum[4][:, :])
                        return e.tensor_copy(v_r[:, gt % 6, 512:1024],
                                             psum[5][:, :])
                    em.op("dve", mkv, ins=[PS[4], PS[5]], outs=[VRING])
                    def mkq2(e, sqb=sqb):
                        e.activation(sqb[:, 0:512], psum[0][:, :], AF.Square)
                        return e.activation(sqb[:, 512:1024], psum[1][:, :],
                                            AF.Square)
                    em.op("act", mkq2, ins=[PS[0], PS[1]], outs=[SQSB])
                    em.op("dve", lambda e, rcb=rcb, sqb=sqb: e.tensor_reduce(
                        rcb[:, :], sqb.rearrange("p (h d) -> p h d", h=16),
                        AX.X, ALU.add), ins=[SQSB], outs=[RCOLB])
                    em.op("act", lambda e, rcb=rcb: e.activation(
                        rcb[:, :], rcb[:, :], AF.Sqrt),
                        ins=[RCOLB], outs=[RCOLB])
                    em.op("dve", lambda e, rcb=rcb: e.tensor_single_scalar(
                        rcb[:, :], rcb[:, :], 1e-12, ALU.max),
                        ins=[RCOLB], outs=[RCOLB])
                    em.op("dve", lambda e, rcb=rcb: e.reciprocal(
                        rcb[:, :], rcb[:, :]), ins=[RCOLB], outs=[RCOLB])
                    em.op("dve", lambda e, i=i, rcb=rcb: e.tensor_single_scalar(
                        rsq8_r[:, i, :], rcb[:, :], 8.0, ALU.mult),
                        ins=[RCOLB], outs=[RSQ8])
                    em.op("dve", lambda e, i=i, rcb=rcb: e.tensor_single_scalar(
                        nrsq8_r[:, i, :], rcb[:, :], -8.0, ALU.mult),
                        ins=[RCOLB], outs=[NRSQ8])
                    def mkk2(e, sqb=sqb):
                        e.activation(sqb[:, 0:512], psum[2][:, :], AF.Square)
                        return e.activation(sqb[:, 512:1024], psum[3][:, :],
                                            AF.Square)
                    em.op("act", mkk2, ins=[PS[2], PS[3]], outs=[SQSB])
                    em.op("dve", lambda e, rskb=rskb, sqb=sqb: e.tensor_reduce(
                        rskb[:, :], sqb.rearrange("p (h d) -> p h d", h=16),
                        AX.X, ALU.add), ins=[SQSB], outs=[RSKB])
                    em.op("act", lambda e, rskb=rskb: e.activation(
                        rskb[:, :], rskb[:, :], AF.Sqrt),
                        ins=[RSKB], outs=[RSKB])
                    em.op("dve", lambda e, rskb=rskb: e.tensor_single_scalar(
                        rskb[:, :], rskb[:, :], 1e-12, ALU.max),
                        ins=[RSKB], outs=[RSKB])
                    em.op("dve", lambda e, rskb=rskb: e.reciprocal(
                        rskb[:, :], rskb[:, :]), ins=[RSKB], outs=[RSKB])
                    def mkkh(e, k1bb=k1bb, rskb=rskb):
                        for s in range(2):
                            e.tensor_tensor(
                                k1bb[:, s * 512:(s + 1) * 512].rearrange(
                                    "p (h d) -> p h d", h=8),
                                psum[2 + s][:, :].rearrange(
                                    "p (h d) -> p h d", h=8),
                                rskb[:, s * 8:(s + 1) * 8].unsqueeze(2)
                                .broadcast_to([128, 8, 64]), ALU.mult)
                        return e.engine_nop()
                    em.op("dve", mkkh, ins=[PS[2], PS[3], RSKB], outs=[K1BB])
                    def mkkr1(e, gt=gt, krb=krb, k1bb=k1bb):
                        return e.tensor_tensor(
                            krb.rearrange("p (h d) -> p h d", h=16),
                            k1bb.rearrange("p (h d) -> p h d", h=16),
                            csk_r[:, gt, :].unsqueeze(1)
                            .broadcast_to([128, 16, 64]), ALU.mult)
                    em.op("pool", mkkr1, ins=[K1BB, TBL], outs=[KRB])
                    def mkkr2(e, gt=gt, t2bb=t2bb, k1bb=k1bb):
                        k3 = k1bb.rearrange("p (h d) -> p h d", h=16)
                        t3 = t2bb.rearrange("p (h d) -> p h d", h=16)
                        e.tensor_tensor(t3[:, :, 0:32], k3[:, :, 32:64],
                                        ssk_r[:, gt, 0:32].unsqueeze(1)
                                        .broadcast_to([128, 16, 32]), ALU.mult)
                        return e.tensor_tensor(
                            t3[:, :, 32:64], k3[:, :, 0:32],
                            ssk_r[:, gt, 32:64].unsqueeze(1)
                            .broadcast_to([128, 16, 32]), ALU.mult)
                    em.op("pool", mkkr2, ins=[K1BB, TBL], outs=[T2BB])
                    em.op("pool", lambda e, krb=krb, t2bb=t2bb: e.tensor_tensor(
                        krb[:, :], krb[:, :], t2bb[:, :], ALU.add),
                        ins=[KRB, T2BB], outs=[KRB])
                    def mkqr1(e, gt=gt, qrb=qrb):
                        for s in range(2):
                            e.tensor_tensor(
                                qrb[:, s * 512:(s + 1) * 512].rearrange(
                                    "p (h d) -> p h d", h=8),
                                psum[s][:, :].rearrange("p (h d) -> p h d", h=8),
                                csq_r[:, gt, :].unsqueeze(1)
                                .broadcast_to([128, 8, 64]), ALU.mult)
                        return e.engine_nop()
                    em.op("dve", mkqr1, ins=[PS[0], PS[1], TBL], outs=[QRB])
                    def mkqr2(e, gt=gt, t2bb=t2bb):
                        t3 = t2bb.rearrange("p (h d) -> p h d", h=16)
                        for s in range(2):
                            q3 = psum[s][:, :].rearrange("p (h d) -> p h d", h=8)
                            e.tensor_tensor(t3[:, s * 8:(s + 1) * 8, 0:32],
                                            q3[:, :, 32:64],
                                            ssq_r[:, gt, 0:32].unsqueeze(1)
                                            .broadcast_to([128, 8, 32]),
                                            ALU.mult)
                            e.tensor_tensor(t3[:, s * 8:(s + 1) * 8, 32:64],
                                            q3[:, :, 0:32],
                                            ssq_r[:, gt, 32:64].unsqueeze(1)
                                            .broadcast_to([128, 8, 32]),
                                            ALU.mult)
                        return e.engine_nop()
                    em.op("dve", mkqr2, ins=[PS[0], PS[1], TBL], outs=[T2BB])
                    em.op("dve", lambda e, qrb=qrb, t2bb=t2bb: e.tensor_tensor(
                        qrb[:, :], qrb[:, :], t2bb[:, :], ALU.add),
                        ins=[QRB, T2BB], outs=[QRB])
                    if cfg.get("attn_taps") and b == 0:
                        store("sp", qkvtap_h[0, gt, :, :], qrb[:, :], QRB,
                              B("qkvtap"))
                        store("sp", qkvtap_h[1, gt, :, :], krb[:, :], KRB,
                              B("qkvtap"))
                        def mkvt(e, gt=gt):
                            return e.dma_start(out=qkvtap_h[2, gt, :, :],
                                               in_=v_r[:, gt % 6, :])
                        em.op("sp", mkvt, ins=[VRING], outs=[B("qkvtap")],
                              dma=True, sem="d_st_vtap")
                    transpose_pack(qrb, QRB, 8,
                                   qTb_r[:, :, i * 128:(i + 1) * 128], QTB)
                    transpose_pack(krb, KRB, 8,
                                   kT_r[:, :, 256 + i * 128:256 + (i + 1) * 128],
                                   KTR)
                for wloc in range(2):
                    w = 2 * g + wloc
                    keyoff = wloc * 256
                    for sub in range(2):
                        i = wloc * 2 + sub
                        gt = g * 4 + i
                        avb, av_rb, AVB = av[sub], av_r[sub], AV[sub]
                        if w > 0:
                            vs, ve = sub * 128, 384 + sub * 128
                            masks = [(tril_klt, B("C_TRIL_KLT"), sub * 128),
                                     (triu_kgt, B("C_TRIU_KGT"),
                                      256 + sub * 128)]
                        else:
                            vs, ve = 256, 384 + sub * 128
                            masks = [(triu_kgt, B("C_TRIU_KGT"),
                                      256 + sub * 128)]
                        for hp in range(8):
                            par = hp % 2
                            att, ATTB = attns[sub][par], ATT[sub][par]
                            sc2, SC2B = scol[par], SCOL[par]
                            dg, DGB = diag[par], DIAG[par]
                            atT, ATT_TB = attnT[par], ATT_T[par]
                            sb0 = 2 * par
                            def mksc(e, hp=hp, vs=vs, ve=ve, masks=masks, sb0=sb0,
                                     sub=sub, wloc=wloc, keyoff=keyoff,
                                     qTb_r=qTb_r):
                                last = None
                                for hh in range(2):
                                    r0, r1 = hh * 64, hh * 64 + 64
                                    e.matmul(
                                        psum[sb0 + hh][:, vs:ve],
                                        qTb_r[r0:r1, hp,
                                              wloc * 256 + sub * 128:
                                              wloc * 256 + sub * 128 + 128],
                                        kT_r[r0:r1, hp,
                                             keyoff + vs:keyoff + ve],
                                        start=True, stop=False)
                                    for mi, (mt, mb, mc) in enumerate(masks):
                                        last = e.matmul(
                                            psum[sb0 + hh][:, mc:mc + 128],
                                            mt[:, :], negeye[:, :],
                                            start=False,
                                            stop=(mi == len(masks) - 1))
                                return last
                            def _sc_outs(sb0=sb0):
                                return [PS[sb0], PS[sb0 + 1]]
                            em.op("pe", mksc,
                                  ins=[QTB, KTR, B("C_TRIL_KLT"),
                                       B("C_TRIU_KGT"), B("C_NEGEYE")],
                                  outs=_sc_outs())
                            for hh in range(2):
                                h = 2 * hp + hh
                                def mkcp(e, hh=hh, vs=vs, ve=ve, att=att,
                                         sb0=sb0):
                                    return e.activation(att[:, vs:ve],
                                                        psum[sb0 + hh][:, vs:ve],
                                                        AF.Copy)
                                em.op("act", mkcp, ins=[PS[sb0 + hh]],
                                      outs=[ATTB])
                                if (cfg.get("attn_taps") and b == 0 and w == 1
                                        and sub == 0 and hp == 0 and hh == 0):
                                    def mkrt(e, att=att):
                                        return e.dma_start(out=attap_h[2, :, :],
                                                           in_=att[:, :])
                                    em.op("sp", mkrt, ins=[ATTB],
                                          outs=[B("attap")], dma=True,
                                          sem="d_st_attap")
                                def mkmax(e, vs=vs, ve=ve, att=att, sc2=sc2):
                                    return e.tensor_reduce(
                                        sc2[:, 0:1], att[:, vs:ve], AX.X,
                                        ALU.max)
                                em.op("dve", mkmax, ins=[ATTB], outs=[SC2B])
                                def mkbias(e, i=i, h=h, sc2=sc2):
                                    return e.tensor_tensor(
                                        sc2[:, 1:2], sc2[:, 0:1],
                                        nrsq8_r[:, i, h:h + 1], ALU.mult)
                                em.op("dve", mkbias, ins=[SC2B, NRSQ8],
                                      outs=[SC2B])
                                def mkexp(e, hh=hh, vs=vs, ve=ve, att=att,
                                          i=i, h=h, sc2=sc2, sb0=sb0):
                                    return e.activation(
                                        att[:, vs:ve], psum[sb0 + hh][:, vs:ve],
                                        AF.Exp, bias=sc2[:, 1:2],
                                        scale=rsq8_r[:, i, h:h + 1],
                                        accum_out=sc2[:, 2:3])
                                em.op("act", mkexp,
                                      ins=[PS[sb0 + hh], SC2B, RSQ8],
                                      outs=[ATTB, SC2B])
                                em.op("dve", lambda e, sc2=sc2: e.reciprocal(
                                    sc2[:, 3:4], sc2[:, 2:3]),
                                    ins=[SC2B], outs=[SC2B])
                                em.op("dve", lambda e, dg=dg, sc2=sc2:
                                      e.tensor_scalar(
                                          dg[:, :], ident[:], sc2[:, 3:4],
                                          None, ALU.mult),
                                      ins=[SC2B, B("C_IDENT")], outs=[DGB])
                                def mkat(e, att=att, dg=dg, par=par):
                                    last = None
                                    for kc in range(4):
                                        last = e.matmul(
                                            psum[4 + par][:, kc * 128:(kc + 1) * 128],
                                            att[:, kc * 128:(kc + 1) * 128],
                                            dg[:, :], start=True, stop=True)
                                    return last
                                em.op("pe", mkat, ins=[ATTB, DGB],
                                      outs=[PS[4 + par]])
                                em.op("dve", lambda e, atT=atT, par=par:
                                      e.tensor_copy(atT[:, :],
                                                    psum[4 + par][:, :]),
                                      ins=[PS[4 + par]], outs=[ATT_TB])
                                if (cfg.get("attn_taps") and b == 0
                                        and w == 1 and sub == 0 and hp == 0
                                        and hh == 0):
                                    def mkat1(e, att=att):
                                        return e.dma_start(
                                            out=attap_h[0, :, :],
                                            in_=att[:, :])
                                    em.op("sp", mkat1, ins=[ATTB],
                                          outs=[B("attap")], dma=True,
                                          sem="d_st_attap")
                                    def mkat2(e):
                                        return e.dma_start(
                                            out=attap_h[1, :, :],
                                            in_=attnT[:, :])
                                    em.op("sp", mkat2, ins=[ATT_T],
                                          outs=[B("attap")], dma=True,
                                          sem="d_st_attap")
                                def mkav(e, w=w, h=h, hh=hh, atT=atT,
                                         par=par):
                                    last = None
                                    for kc in range(4):
                                        slot = (2 * w - 2 + kc) % 6
                                        last = e.matmul(
                                            psum[6 + par][0:64,
                                                    hh * 128:hh * 128 + 128],
                                            v_r[:, slot, h * 64:(h + 1) * 64],
                                            atT[:, kc * 128:(kc + 1) * 128],
                                            start=(kc == 0), stop=(kc == 3))
                                    return last
                                em.op("pe", mkav, ins=[ATT_TB, VRING],
                                      outs=[PS[6 + par]])
                            def mkave(e, hp=hp, av_rb=av_rb, par=par):
                                e.tensor_copy(av_rb[0:64, hp, :],
                                              psum[6 + par][0:64, 0:128])
                                return e.tensor_copy(av_rb[64:128, hp, :],
                                                     psum[6 + par][0:64,
                                                                   128:256])
                            em.op("dve", mkave, ins=[PS[6 + par]], outs=[AVB])
                        if cfg.get("attn_taps") and b == 0:
                            def mkavt(e, w=w, sub=sub, avb=avb):
                                return e.dma_start(
                                    out=avtap_h[w * 2 + sub, :, :],
                                    in_=avb[:, :])
                            em.op("sp", mkavt, ins=[AVB], outs=[B("avtap")],
                                  dma=True, sem="d_st_avtap")
                        t0 = w * 256 + sub * 128
                        hb, HB = ht[i], HT[i]
                        hn, HNB = hnew[sub], HN[sub]
                        wb0 = 2 * sub
                        def mkwo(e, av_rb=av_rb, wb0=wb0):
                            last = None
                            for k in range(8):
                                for s in range(2):
                                    last = e.matmul(
                                        psum[wb0 + s][:, :], av_rb[:, k, :],
                                        wo_r[:, k, s * 512:(s + 1) * 512],
                                        start=(k == 0), stop=(k == 7))
                            return last
                        em.op("pe", mkwo, ins=[AVB, WOB],
                              outs=[PS[wb0], PS[wb0 + 1]])
                        def mkres(e, hn=hn, hb=hb, wb0=wb0):
                            for s in range(2):
                                e.tensor_tensor(hn[:, s * 512:(s + 1) * 512],
                                                psum[wb0 + s][:, :],
                                                hb[:, s * 512:(s + 1) * 512],
                                                ALU.add)
                            return e.engine_nop()
                        em.op("dve", mkres, ins=[PS[wb0], PS[wb0 + 1], HB],
                              outs=[HNB])
                        store("sp", hdst[b, t0:t0 + 128, :], hn[:, :], HNB,
                              HDST)
                em.op("pool", lambda e: e.tensor_copy(
                    kT_r[:, :, 0:256], kT_r[:, :, 512:768]),
                    ins=[KTR], outs=[KTR])
        em.barrier()
        arena.release()

    def phase_ffnA(l, hsrc, HSRC):
        arena.mark()
        w1 = arena.bf16(8 * 2 * DFF)
        w1_r = w1.rearrange("p (k e) -> p k e", k=8)
        W1B = B("w1b")
        for k in range(8):
            em.op("pool", lambda e, k=k: e.dma_start(
                out=w1_r[:, k, :], in_=W1[l, k * 128:(k + 1) * 128, :]),
                ins=[], outs=[W1B], dma=True, sem="d_w1b", waw=(k == 0))
        gcol = arena.f32(8)
        GCOL = B("f_gcol")
        dma("sp", gcol[:, :], ln_g[l].rearrange("(c p) -> p c", p=128),
            outs=[GCOL], allow_slow_non_contiguous=True)
        bcol = arena.bf16(8)
        BCOL = B("f_bcol")
        dma("pool", bcol[:, :], ln_b[l].rearrange("(c p) -> p c", p=128),
            outs=[BCOL], allow_slow_non_contiguous=True)
        b1row = arena.f32(2 * DFF)
        B1R = B("f_b1row")
        dma("sp", b1row[0:1, :], b1[l:l + 1, :], outs=[B1R])
        brow_sb = arena.f32(2 * DFF)
        BRS = B("f_brs")
        for j in range(11):
            n0, n1 = j * 512, min(j * 512 + 512, 2 * DFF)
            def mkb(e, j=j, n0=n0, n1=n1):
                last = None
                for k in range(8):
                    last = e.matmul(psum[j % 2][0:1, 0:n1 - n0],
                                    bcol[:, k:k + 1], w1_r[:, k, n0:n1],
                                    start=(k == 0), stop=(k == 7))
                return last
            em.op("pe", mkb, ins=[BCOL, W1B], outs=[PS[j % 2]])
            def mkbe(e, j=j, n0=n0, n1=n1):
                return e.tensor_copy(brow_sb[0:1, n0:n1],
                                     psum[j % 2][0:1, 0:n1 - n0])
            em.op("dve", mkbe, ins=[PS[j % 2]], outs=[BRS])
        em.op("dve", lambda e: e.tensor_tensor(
            brow_sb[0:1, :], brow_sb[0:1, :], b1row[0:1, :], ALU.add),
            ins=[BRS, B1R], outs=[BRS])
        store("sp", brow_h[0:2 * DFF].unsqueeze(0), brow_sb[0:1, :],
              BRS, B("brow_h"))
        b1c = arena.f32(44)
        B1C = B("f_b1c")
        dma("sp", b1c[:, 0:22],
            brow_h[0:2816].rearrange("(c p) -> p c", p=128),
            outs=[B1C], ins=[B("brow_h")], allow_slow_non_contiguous=True)
        dma("sp", b1c[:, 22:44],
            brow_h[DFF:DFF + 2816].rearrange("(c p) -> p c", p=128),
            outs=[B1C], ins=[B("brow_h")], allow_slow_non_contiguous=True)
        for k in range(8):
            def mkg(e, k=k):
                return e.tensor_scalar(w1_r[:, k, :], w1_r[:, k, :],
                                       gcol[:, k:k + 1], None, ALU.mult)
            em.op("pool", mkg, ins=[W1B, GCOL], outs=[W1B])

        ht = [arena.f32(1024) for _ in range(2)]
        HT2 = [B("f_h0"), B("f_h1")]
        hsh = [arena.f32(512) for _ in range(2)]
        HSH2 = [B("f_hs0"), B("f_hs1")]
        xn = [arena.bf16(1024) for _ in range(2)]
        XN2 = [B("f_xn0"), B("f_xn1")]
        xnT = [arena.bf16(8 * 512) for _ in range(2)]
        xnT_r = [t.rearrange("p (k t) -> p k t", k=8) for t in xnT]
        XNT2 = [B("f_xnT0"), B("f_xnT1")]
        tg = [arena.bf16(512) for _ in range(2)]
        TG = [B("f_tg0"), B("f_tg1")]
        shtile = [arena.f32(1024) for _ in range(2)]
        SHT = [B("f_sht0"), B("f_sht1")]
        gact = [arena.bf16(512) for _ in range(2)]
        GACT = [B("f_g0"), B("f_g1")]

        for b in range(Bc):
            for g in range(NG):
                xnTb_r, XNTB = xnT_r[g % 2], XNT2[g % 2]
                for i in range(4):
                    t0 = g * 512 + i * 128
                    hb, HB = ht[i % 2], HT2[i % 2]
                    hs, HS = hsh[i % 2], HSH2[i % 2]
                    dma("sp", hb[:, :], hsrc[b, t0:t0 + 128, :], outs=[HB],
                        ins=[HSRC])
                    if t0 == 0:
                        em.op("dve", lambda e, hs=hs: e.memset(hs[0:1, :], 0.0),
                              outs=[HS])
                        dma("sp", hs[1:128, :], hsrc[b, 0:127, 512:1024],
                            outs=[HS], ins=[HSRC])
                    else:
                        dma("sp", hs[:, :], hsrc[b, t0 - 1:t0 + 127, 512:1024],
                            outs=[HS], ins=[HSRC])
                    shb, SHB = shtile[i % 2], SHT[i % 2]
                    xnb2, XNB2 = xn[i % 2], XN2[i % 2]
                    em.op("dve", lambda e, hb=hb, shb=shb: e.tensor_copy(
                        shb[:, 0:512], hb[:, 0:512]), ins=[HB], outs=[SHB])
                    em.op("dve", lambda e, hs=hs, shb=shb: e.tensor_copy(
                        shb[:, 512:1024], hs[:, :]), ins=[HS], outs=[SHB])
                    ln_tile(shb[:, :], xnb2[:, :], SHB, XNB2)
                    transpose_pack(xnb2, XNB2, 8,
                                   xnTb_r[:, :, i * 128:(i + 1) * 128], XNT2[g % 2])
                for j in range(22):
                    a0 = j * 128
                    nr = min(128, DFF - a0)
                    g0 = DFF + a0
                    bkA, bkG = 2 * (j % 2), 2 * (j % 2) + 1
                    def mkw1(e, a0=a0, g0=g0, nr=nr, bkA=bkA, bkG=bkG,
                             xnTb_r=xnTb_r):
                        last = None
                        for k in range(8):
                            e.matmul(psum[bkA][0:nr, :],
                                     w1_r[:, k, a0:a0 + nr], xnTb_r[:, k, :],
                                     start=(k == 0), stop=(k == 7))
                            last = e.matmul(psum[bkG][0:nr, :],
                                            w1_r[:, k, g0:g0 + nr],
                                            xnTb_r[:, k, :],
                                            start=(k == 0), stop=(k == 7))
                        return last
                    em.op("pe", mkw1, ins=[XNT2[g % 2], W1B],
                          outs=[PS[bkA], PS[bkG]])
                    tgb, TGB = tg[j % 2], TG[j % 2]
                    def mkgel2(e, j=j, nr=nr, bkG=bkG, tgb=tgb):
                        return e.activation(tgb[0:nr, :], psum[bkG][0:nr, :],
                                            AF.Gelu,
                                            bias=b1c[0:nr, 22 + j:23 + j])
                    em.op("act", mkgel2, ins=[PS[bkG], B1C], outs=[TGB])
                    gb, GB = gact[j % 2], GACT[j % 2]
                    def mkga(e, j=j, nr=nr, bkA=bkA, gb=gb, tgb=tgb):
                        return e.scalar_tensor_tensor(
                            gb[0:nr, :], psum[bkA][0:nr, :],
                            b1c[0:nr, j:j + 1], tgb[0:nr, :], ALU.add, ALU.mult)
                    em.op("dve", mkga, ins=[PS[bkA], B1C, TGB], outs=[GB])
                    def mkgs(e, j=j, nr=nr, b=b, g=g, gb=gb):
                        return e.dma_start(
                            out=g_h[b, j * 128:j * 128 + nr,
                                    g * 512:(g + 1) * 512],
                            in_=gb[0:nr, :])
                    em.op("sp", mkgs, ins=[GB], outs=[B("g_h")], dma=True,
                          sem="d_st_" + GB.name, waw=False)
        em.barrier()
        arena.release()

    def phase_ffnB(l, hsrc, hdst, HSRC, HDST, final=False):
        arena.mark()
        w2 = arena.bf16(22 * 1024)
        w2_r = w2.rearrange("p (k e) -> p k e", k=22)
        W2B = B("w2b")
        for k in range(22):
            r0, r1 = k * 128, min(k * 128 + 128, DFF)
            em.op("pool", lambda e, k=k, r0=r0, r1=r1: e.dma_start(
                out=w2_r[0:r1 - r0, k, :], in_=W2[l, r0:r1, :]),
                ins=[], outs=[W2B], dma=True, sem="d_w2b", waw=(k == 0))
        b2row = arena.bf16(1024)
        B2R = B("f_b2row")
        dma("pool", b2row[0:1, :], b2[l:l + 1, :], outs=[B2R], sem="d_b2row")
        gsb = [arena.bf16(22 * 128) for _ in range(2)]
        gsb_r = [t.rearrange("p (k t) -> p k t", k=22) for t in gsb]
        GSB = [B("f2_g0"), B("f2_g1")]
        ht = [arena.f32(1024) for _ in range(2)]
        HT3 = [B("f2_h0"), B("f2_h1")]
        hn = [arena.f32(1024) for _ in range(2)]
        HN3 = [B("f2_hn0"), B("f2_hn1")]
        x1t = [arena.f32(1024) for _ in range(2)]
        X1T3 = [B("f2_x10"), B("f2_x11")]

        for b in range(Bc):
            for ti in range(NT):
                t0 = ti * 128
                gb_r, GB = gsb_r[ti % 2], GSB[ti % 2]
                hb, HB = ht[ti % 2], HT3[ti % 2]
                hnb, HNB = hn[ti % 2], HN3[ti % 2]
                for k in range(22):
                    r0, r1 = k * 128, min(k * 128 + 128, DFF)
                    def mkgl(e, k=k, r0=r0, r1=r1, b=b, t0=t0, gb_r=gb_r):
                        return e.dma_start(out=gb_r[0:r1 - r0, k, :],
                                           in_=g_h[b, r0:r1, t0:t0 + 128])
                    em.op("sp", mkgl, ins=[B("g_h")], outs=[GB], dma=True,
                          waw=(k == 0))
                dma("sp", hb[:, :], hsrc[b, t0:t0 + 128, :], outs=[HB],
                    ins=[HSRC])
                fb0 = 2 * (ti % 2)
                def mkw2(e, gb_r=gb_r, fb0=fb0):
                    last = None
                    for k in range(22):
                        r1 = min(k * 128 + 128, DFF) - k * 128
                        for s in range(2):
                            e.matmul(psum[fb0 + s][:, :], gb_r[0:r1, k, :],
                                     w2_r[0:r1, k, s * 512:(s + 1) * 512],
                                     start=(k == 0), stop=False)
                    for s in range(2):
                        last = e.matmul(psum[fb0 + s][:, :], ones1[:],
                                        b2row[0:1, s * 512:(s + 1) * 512],
                                        start=False, stop=True)
                    return last
                em.op("pe", mkw2, ins=[GB, W2B, B2R, B("C_ONES1")],
                      outs=[PS[fb0], PS[fb0 + 1]])
                def mkr(e, hnb=hnb, hb=hb, fb0=fb0):
                    for s in range(2):
                        e.tensor_tensor(hnb[:, s * 512:(s + 1) * 512],
                                        psum[fb0 + s][:, :],
                                        hb[:, s * 512:(s + 1) * 512], ALU.add)
                    return e.engine_nop()
                em.op("dve", mkr, ins=[PS[fb0], PS[fb0 + 1], HB], outs=[HNB])
                if not final:
                    store("sp", hdst[b, t0:t0 + 128, :], hnb[:, :], HNB, HDST)
                else:
                    x1b, X1B = x1t[ti % 2], X1T3[ti % 2]
                    dma("sp", x1b[:, :], x1_h[b, t0:t0 + 128, :], outs=[X1B],
                        ins=[B("x1_h")])
                    ln_tile(hnb[:, :], hnb[:, :], HNB, HNB)
                    em.op("dve", lambda e, hnb=hnb, x1b=x1b: e.tensor_tensor(
                        hnb[:, :], hnb[:, :], x1b[:, :], ALU.add),
                        ins=[HNB, X1B], outs=[HNB])
                    store("sp", OUT[b, t0:t0 + 128, :], hnb[:, :], HNB,
                          B("OUT"))
        em.barrier()
        arena.release()

    phases = cfg.get("phases", ("m1", "m2"))
    if "m1" in phases:
        phase_m1a()
        phase_m1b()
    if "m2" in phases:
        phase_mamba2()
    if "tf" in phases:
        hsrcs = [(hA_h, B("hA_h")), (hB_h, B("hB_h"))]
        for l in range(depth):
            a, bb = hsrcs[0], hsrcs[1]
            phase_attn(l, a[0], bb[0], a[1], bb[1])
            phase_ffnA(l, bb[0], bb[1])
            phase_ffnB(l, bb[0], a[0], bb[1], a[1], final=(l == depth - 1))

    for name, (dst, srcap) in dbg.items():
        shape = srcap.tensor.shape
        if len(shape) == 3:
            for b in range(shape[0]):
                for r0 in range(0, shape[1], 512):
                    r1 = min(r0 + 512, shape[1])
                    dma("sp", dst[b, r0:r1, :], srcap[b, r0:r1, :],
                        sem="d_tap")
        else:
            for r0 in range(0, shape[0], 512):
                r1 = min(r0 + 512, shape[0])
                dma("sp", dst[r0:r1, :], srcap[r0:r1, :], sem="d_tap")
    em.barrier()

    import contextlib
    with contextlib.ExitStack() as stack:
        semnames = sorted(em.done.keys())
        assert len(semnames) <= 200, f"too many semaphores: {len(semnames)}"
        sems = {s: stack.enter_context(nc.semaphore("sem_" + s))
                for s in semnames}
        block = stack.enter_context(nc.Block())

        def runner(ename):
            def run(e):
                for waits, fn, s, inc in em.cmds[ename]:
                    for ds, dv in waits.items():
                        e.wait_ge(sems[ds], dv)
                    if fn is not None:
                        last = fn(e)
                        if inc:
                            last.then_inc(sems[s], inc)
            return run
        block.tensor(runner("pe"))
        block.vector(runner("dve"))
        block.scalar(runner("act"))
        block.gpsimd(runner("pool"))
        block.sync(runner("sp"))

    class P:
        pass
    p = P()
    p.nc = nc
    p.const_arrays = consts
    p.em = em
    return p


def kernel(**inputs):
    cfg = {"L": 4096, "B": 2, "depth": 6, "phases": ("m1", "m2", "tf"),
           "debug": ()}
    p = build_program(cfg)
    n = 8
    in_maps = []
    for c in range(n):
        m = {}
        for k, v in inputs.items():
            if k == "L0_attn_u":
                m[k] = np.ascontiguousarray(v[2 * c:2 * c + 2])
            else:
                m[k] = np.ascontiguousarray(v)
        for k, v in p.const_arrays.items():
            m[k] = v
        in_maps.append(m)
    res = run_bass_kernel_spmd(p.nc, in_maps, core_ids=list(range(n)))
    return np.concatenate([r["OUT"] for r in res.results], axis=0)
